# version 10
# baseline (speedup 1.0000x reference)
import contextlib
import numpy as np

import concourse.bass as bass
import concourse.mybir as mybir
from concourse.bass_utils import run_bass_kernel_spmd
from concourse.bass_core_rust import IndirectOffsetOnAxis

F32 = mybir.dt.float32
BF16 = mybir.dt.bfloat16
I32 = mybir.dt.int32
AF = mybir.ActivationFunctionType
ALU = mybir.AluOpType
AX = mybir.AxisListType.X

D, DI, NST, DTR, NE, HE = 1024, 2048, 16, 64, 8, 4096
SW = 512
SST = SW - 3
TB = 512
ENGS = ("sync", "scalar", "vector", "tensor", "gpsimd")


class Pb:

    def __init__(self):
        self.ops = {e: [] for e in ENGS}
        self.cnt = {}
        self.hw = {}
        self.raw = {}
        self.war = {}

    def _sem(self, name):
        self.cnt.setdefault(name, 0)

    def wait(self, eng, tok):
        if tok is None:
            return
        s, v = tok
        if self.hw.get((eng, s), 0) >= v:
            return
        self.hw[(eng, s)] = v
        self.ops[eng].append(("w", s, v))

    def run(self, eng, fn, inc=None, n=1):
        tok = None
        if inc is not None:
            self._sem(inc)
            self.cnt[inc] += n
            tok = (inc, self.cnt[inc])
        self.ops[eng].append(("r", fn, inc, n))
        return tok

    def mark(self, eng, sem, n=1):
        self._sem(sem)
        self.cnt[sem] += n
        self.ops[eng].append(("i", sem, n))
        return (sem, self.cnt[sem])

    def rd(self, eng, *keys):
        for k in keys:
            self.wait(eng, self.raw.get(k))

    def rdone(self, tok, *keys):
        if tok is None:
            return
        for k in keys:
            self.war.setdefault(k, []).append(tok)

    def wr(self, eng, *keys):
        for k in keys:
            for t in self.war.get(k, []):
                self.wait(eng, t)
            self.wait(eng, self.raw.get(k))

    def wdone(self, tok, *keys):
        for k in keys:
            self.raw[k] = tok
            self.war[k] = []


def _stripes(L):
    out = []
    t0 = 0
    while t0 < L:
        out.append((t0, min(SST, L - t0)))
        t0 += SST
    return out


def build(L):
    WIN = L // 2
    nblocks = L // TB
    nrounds = nblocks // 2
    stripes = _stripes(L)
    ns = len(stripes)
    NWG = WIN // 128

    nc = bass.Bass("TRN2", target_bir_lowering=False)

    x_d = nc.dram_tensor("x", [L, D], F32, kind="ExternalInput").ap()
    xres_d = nc.dram_tensor("xres", [WIN, D], F32, kind="ExternalInput").ap()
    idx_d = nc.dram_tensor("idx", [128, NWG], I32, kind="ExternalInput").ap()
    Win_d = nc.dram_tensor("W_in", [D, DI], F32, kind="ExternalInput").ap()
    cwx_d = nc.dram_tensor("cwx", [128, 32], F32, kind="ExternalInput").ap()
    cbx_d = nc.dram_tensor("cbx", [128, 8], F32, kind="ExternalInput").ap()
    cwz_d = nc.dram_tensor("cwz", [128, 32], F32, kind="ExternalInput").ap()
    cbz_d = nc.dram_tensor("cbz", [128, 8], F32, kind="ExternalInput").ap()
    Wxd_d = nc.dram_tensor("W_xdbl", [D, 96], F32, kind="ExternalInput").ap()
    Wdt_d = nc.dram_tensor("W_dt", [DTR, D], F32, kind="ExternalInput").ap()
    inv_d = nc.dram_tensor("invdt", [128, 8], F32, kind="ExternalInput").ap()
    Dv_d = nc.dram_tensor("Dv", [128, 8], F32, kind="ExternalInput").ap()
    Wout_d = nc.dram_tensor("W_out", [DI, D], F32, kind="ExternalInput").ap()
    bout_d = nc.dram_tensor("bout", [128, 8], F32, kind="ExternalInput").ap()
    rw1_d = nc.dram_tensor("rw1", [D, D], F32, kind="ExternalInput").ap()
    rb1_d = nc.dram_tensor("rb1", [128, 8], F32, kind="ExternalInput").ap()
    rw2_d = nc.dram_tensor("rw2", [D, NE], F32, kind="ExternalInput").ap()
    rb2_d = nc.dram_tensor("rb2", [NE, 1], F32, kind="ExternalInput").ap()
    we1_d = nc.dram_tensor("we1", [NE, D, HE], F32, kind="ExternalInput").ap()
    be1_d = nc.dram_tensor("be1c", [128, NE * 32], F32, kind="ExternalInput").ap()
    we2_d = nc.dram_tensor("we2", [NE, HE, D], F32, kind="ExternalInput").ap()
    be2_d = nc.dram_tensor("be2", [NE, D], F32, kind="ExternalInput").ap()
    id_d = nc.dram_tensor("ident", [128, 128], F32, kind="ExternalInput").ap()
    out_d = nc.dram_tensor("out", [WIN, D], F32, kind="ExternalOutput").ap()

    att_s = nc.dram_tensor("att_s", [L, D], BF16, kind="Internal").ap()
    xdb_s = nc.dram_tensor("xdb_s", [32, L], F32, kind="Internal").ap()
    win_s = nc.dram_tensor("win_s", [D, DI], BF16, kind="Internal").ap()
    wout_s = nc.dram_tensor("wout_s", [DI, D], BF16, kind="Internal").ap()
    rw1_s = nc.dram_tensor("rw1_s", [D, D], BF16, kind="Internal").ap()
    we1_s = nc.dram_tensor("we1_s", [NE, D, HE], BF16, kind="Internal").ap()
    we2_s = nc.dram_tensor("we2_s", [NE, HE, D], BF16, kind="Internal").ap()

    es = contextlib.ExitStack()
    sb = lambda nm, sh, dt: es.enter_context(nc.sbuf_tensor(nm, sh, dt))

    c_idx = sb("c_idx", [128, NWG], I32)
    c_cwx = sb("c_cwx", [128, 32], F32)
    c_cbx = sb("c_cbx", [128, 8], F32)
    c_cwz = sb("c_cwz", [128, 32], F32)
    c_cbz = sb("c_cbz", [128, 8], F32)
    c_inv = sb("c_inv", [128, 8], F32)
    c_inv2 = sb("c_inv2", [128, 8], F32)
    c_D = sb("c_D", [128, 8], F32)
    c_bout = sb("c_bout", [128, 8], F32)
    c_rb1 = sb("c_rb1", [128, 8], F32)
    c_rb2 = sb("c_rb2", [NE, 1], F32)
    c_be1 = sb("c_be1", [128, NE * 32], F32)
    c_id32 = sb("c_id32", [128, 128], F32)
    c_idbf = sb("c_idbf", [128, 128], BF16)
    w_xdbl = sb("w_xdbl", [128, 8 * 96], BF16)
    w_dt = sb("w_dt", [DTR, D], BF16)
    w_rw2 = sb("w_rw2", [128, 8 * NE], BF16)
    w_be2 = sb("w_be2", [NE, D], BF16)

    xst = [sb("xst0", [128, 1024], BF16)]
    attm = [sb(f"attm{i}", [128, 1024], BF16) for i in range(2)]
    xfm = sb("xfm", [128, 8 * SW], BF16)
    xsb = sb("xsb", [128, 8 * SW], BF16)
    zb = sb("zb", [128, 8 * SW], BF16)
    ub = sb("ub", [128, 8 * SST], BF16)
    zcb = sb("zcb", [128, 8 * SST], BF16)
    cacc = [sb("cacc0", [128, SW], BF16), None]
    rvt = sb("rvt", [128, SST], F32)
    dtmp = [sb("dtmp0", [128, SST], F32), None]
    a_r = [sb(f"a_r{i}", [128, 2 * SST], F32) for i in range(2)]
    b_r = [sb(f"b_r{i}", [128, 2 * SST], BF16) for i in range(2)]
    h_r = [sb(f"h_r{i}", [128, 2 * SST], BF16) for i in range(3)]
    t_r = [sb(f"t_r{i}", [128, 2 * SST], BF16) for i in range(2)]
    bc_r = [sb(f"bc_r{i}", [128, 1024], BF16) for i in range(2)]
    yfm = sb("yfm", [128, 8 * SST], F32)
    carry = [sb(f"carry{i}", [128, 128], F32) for i in range(2)]
    xdbl = sb("xdbl", [128, SW], F32)
    dtlow = sb("dtlow", [DTR, SW], BF16)
    winb = [sb(f"winb{i}", [128, DI], BF16) for i in range(2)]
    woutb = [sb(f"woutb{i}", [128, D], BF16) for i in range(2)]

    attg = [sb("attg0", [128, D], BF16)]
    xresb = [sb("xresb0", [128, D], F32)]
    yacc = sb("yacc", [128, 4 * D], BF16)
    xtf = sb("xtf", [128, 8 * TB], BF16)
    hbuf = sb("hbuf", [128, 32 * TB], BF16)
    w1b = [sb(f"w1b{i}", [128, 8 * 256], BF16) for i in range(4)]
    w2b = [sb(f"w2b{i}", [128, D], BF16) for i in range(3)]
    lgf = sb("lgf", [NE, TB], F32)
    lgt = sb("lgt", [128, 32], F32)
    wtm = sb("wtm", [128, 32], F32)
    wfmb = sb("wfmb", [NE, TB], BF16)
    rs_t = sb("rs_t", [128, 16], F32)
    eqt = sb("eqt", [128, NE], F32)
    ett = sb("ett", [128, NE], F32)
    ewt = sb("ewt", [128, NE], F32)
    outst = [sb(f"outst{i}", [128, D], F32) for i in range(2)]

    pbk = [es.enter_context(nc.psum_tensor(f"pb{i}", [128, 512], F32))
           for i in range(8)]
    pbf = [p.bitcast(BF16) for p in pbk]

    OKEY = ["outst0", "outst1"]

    P = Pb()
    TE, VE, SE, PL, SY, SC = "tensor", "vector", "scalar", "gpsimd", "sync", "scalar"

    def dma(eng, dst, src, sem, key=None, waits=()):
        for t in waits:
            P.wait(eng, t)
        if key:
            P.wr(eng, key)
        tok = P.run(eng, lambda E, d=dst, s=src: E.dma_start(out=d, in_=s),
                    inc=sem, n=16)
        if key:
            P.wdone(tok, key)
        return tok

    consts = [
        (c_idx, idx_d), (c_cwx, cwx_d), (c_cbx, cbx_d), (c_cwz, cwz_d),
        (c_cbz, cbz_d), (c_inv, inv_d), (c_D, Dv_d), (c_bout, bout_d),
        (c_rb1, rb1_d), (c_rb2, rb2_d), (c_be1, be1_d), (c_id32, id_d),
    ]
    for t, s in consts:
        dma(SY, t[:, :], s[:, :], "qconst")
    tok_const = ("qconst", P.cnt["qconst"])

    for k in range(8):
        dma(PL, w_xdbl[:, k * 96:(k + 1) * 96], Wxd_d[k * 128:(k + 1) * 128, :],
            "qsmall")
    dma(PL, w_dt[:, :], Wdt_d[:, :], "qsmall")
    for k in range(8):
        dma(PL, w_rw2[:, k * NE:(k + 1) * NE], rw2_d[k * 128:(k + 1) * 128, :],
            "qsmall")
    dma(PL, w_be2[:, :], be2_d[:, :], "qsmall")
    tok_small = ("qsmall", P.cnt["qsmall"])

    dma(PL, win_s[:, :], Win_d[:, :], "qstgS")
    dma(PL, wout_s[:, :], Wout_d[:, :], "qstgS")
    dma(PL, rw1_s[:, :], rw1_d[:, :], "qstgS")
    tok_stgS = ("qstgS", P.cnt["qstgS"])
    stage_chunks = [(we1_s[e], we1_d[e], we2_s[e], we2_d[e]) for e in range(NE)]

    P.wait(VE, tok_const)
    t = P.run(VE, lambda E: E.tensor_add(c_inv2[:, :], c_inv[:, :], c_inv[:, :]),
              inc="cve")
    P.wdone(t, "c_inv2")
    P.wait(SE, tok_const)
    t = P.run(SE, lambda E: E.copy(c_idbf[:, :], c_id32[:, :]), inc="cse")
    P.wdone(t, "c_idbf")

    def emit_stripe(s, t0, slen, scr_out):
        W = slen + 3
        ngr = (W + 127) // 128

        for g in range(ngr):
            gs = 0
            tk0 = t0 - 1 + g * 128
            r0, r1 = max(tk0, 0), min(tk0 + 128, L)
            p0 = r0 - tk0
            part = (p0 != 0) or (r1 - r0 < 128)
            if part:
                P.wr(VE, f"xst{gs}")
                tm0 = P.run(VE, lambda E, gs=gs: E.memset(xst[gs][:, :], 0.0),
                            inc="cve")
                P.wdone(tm0, f"xst{gs}")
            P.wr(PL, f"xst{gs}")
            tok = P.run(PL, lambda E, r0=r0, r1=r1, p0=p0, gs=gs: E.dma_start(
                out=xst[gs][p0:p0 + (r1 - r0), :], in_=x_d[r0:r1, :]),
                inc=f"qx{gs}", n=16)
            P.wdone(tok, f"xst{gs}")
            P.wait(TE, tok)
            P.rd(TE, f"xst{gs}", "c_idbf")
            P.wr(TE, "pb7")
            for dt in range(8):
                tt = P.run(TE, lambda E, dt=dt, gs=gs: E.matmul(
                    pbf[7][:, dt * 128:(dt + 1) * 128],
                    xst[gs][:, dt * 128:(dt + 1) * 128],
                    c_idbf[:, :], start=(dt == 0), stop=(dt == 7),
                    is_transpose=True), inc="cte" if dt == 7 else None)
            P.wdone(tt, "pb7")
            P.rdone(tt, f"xst{gs}")
            gw = min(W - g * 128, 128)
            for dt in range(8):
                P.rd(SE, "pb7")
                P.wr(SE, f"xfm{dt}")
                te = P.run(SE, lambda E, dt=dt, g=g, gw=gw: E.copy(
                    xfm[:, dt * SW + g * 128:dt * SW + g * 128 + gw],
                    pbf[7][:, dt * 128:dt * 128 + gw]), inc="cse")
                P.rdone(te, "pb7")
                P.wdone(te, f"xfm{dt}")
            yield

        P.wait(SY, tok_stgS)
        for mh in range(2):
            for k in range(8):
                kb = k % 2
                dma(SY, winb[kb][:, :], win_s[k * 128:(k + 1) * 128, :],
                    f"qwin{kb}", key=f"winb{kb}")
                P.rd(TE, f"winb{kb}")
                if k == 0:
                    P.rd(TE, *[f"xfm{d}" for d in range(8)])
                for m8 in range(8):
                    m = mh * 8 + m8
                    if k == 0:
                        P.wr(TE, f"pb{m8}")
                    tt = P.run(TE, lambda E, k=k, m=m, kb=kb, m8=m8, W=W:
                               E.matmul(
                        pbk[m8][:, 0:W],
                        winb[kb][:, m * 128:(m + 1) * 128],
                        xfm[:, k * SW:k * SW + W],
                        start=(k == 0), stop=(k == 7)),
                        inc="cte" if (m8 == 7) else None)
                    if k == 7:
                        P.wdone((("cte", P.cnt["cte"] + 1)
                                 if m8 < 7 else tt), f"pb{m8}")
                P.rdone(tt, f"winb{kb}")
                for d in range(8):
                    P.rdone(tt, f"xfm{d}")
            for m8 in range(8):
                m = mh * 8 + m8
                dst = xsb if m < 8 else zb
                dkey = f"{'xsb' if m < 8 else 'zb'}{m % 8}"
                dcol = (m % 8) * SW
                P.rd(SE, f"pb{m8}")
                P.wr(SE, dkey)
                te = P.run(SE, lambda E, dst=dst, dcol=dcol, m8=m8, W=W: E.copy(
                    dst[:, dcol:dcol + W], pbk[m8][:, 0:W]), inc="cse")
                P.rdone(te, f"pb{m8}")
                P.wdone(te, dkey)
            yield

        for path, src, wsc, bsc, dst, dkey in (
                ("x", xsb, c_cwx, c_cbx, ub, "ub"),
                ("z", zb, c_cwz, c_cbz, zcb, "zcb")):
            skey = "xsb" if path == "x" else "zb"
            for dt in range(8):
                ca = cacc[0]
                ckey = "cacc0"
                P.rd(VE, f"{skey}{dt}")
                P.wait(VE, tok_const)
                P.wr(VE, ckey)
                tv = P.run(VE, lambda E, src=src, dt=dt, wsc=wsc, slen=slen,
                           ca=ca: E.tensor_scalar_mul(
                    ca[:, 0:slen], src[:, dt * SW:dt * SW + slen],
                    wsc[:, dt * 4:dt * 4 + 1]), inc="cve")
                P.wdone(tv, ckey)
                for j in range(1, 4):
                    P.wr(VE, ckey)
                    tv = P.run(VE, lambda E, src=src, dt=dt, j=j, wsc=wsc,
                               slen=slen, ca=ca: E.scalar_tensor_tensor(
                        ca[:, 0:slen],
                        src[:, dt * SW + j:dt * SW + j + slen],
                        wsc[:, dt * 4 + j:dt * 4 + j + 1],
                        ca[:, 0:slen], ALU.mult, ALU.add), inc="cve")
                    P.wdone(tv, ckey)
                P.rdone(tv, f"{skey}{dt}")
                P.rd(SE, ckey)
                P.wr(SE, f"{dkey}{dt}")
                te = P.run(SE, lambda E, dst=dst, dt=dt, bsc=bsc, slen=slen,
                           ca=ca: E.activation(
                    dst[:, dt * SST:dt * SST + slen], ca[:, 0:slen], AF.Silu,
                    bias=bsc[:, dt:dt + 1]), inc="cse")
                P.rdone(te, ckey)
                P.wdone(te, f"{dkey}{dt}")
                if dt % 4 == 3:
                    yield

        P.wait(TE, tok_small)
        for k in range(8):
            P.rd(TE, f"ub{k}")
            if k == 0:
                P.wr(TE, "pb0")
            tt = P.run(TE, lambda E, k=k, slen=slen: E.matmul(
                pbk[0][0:96, 0:slen], w_xdbl[:, k * 96:(k + 1) * 96],
                ub[:, k * SST:k * SST + slen],
                start=(k == 0), stop=(k == 7)), inc="cte")
            P.rdone(tt, f"ub{k}")
        P.wdone(tt, "pb0")
        P.rd(VE, "pb0")
        P.wr(VE, "xdbl")
        tv = P.run(VE, lambda E, slen=slen: E.tensor_copy(
            xdbl[0:96, 0:slen], pbk[0][0:96, 0:slen]), inc="cve")
        P.rdone(tv, "pb0")
        P.wdone(tv, "xdbl")
        P.rd(SE, "xdbl")
        P.wr(SE, "dtlow")
        te = P.run(SE, lambda E, slen=slen: E.copy(
            dtlow[:, 0:slen], xdbl[0:DTR, 0:slen]), inc="cse")
        P.wdone(te, "dtlow")
        P.rd(SC, "xdbl")
        tk = P.run(SC, lambda E, t0=t0, slen=slen: E.dma_start(
            out=xdb_s[:, t0:t0 + slen], in_=xdbl[64:96, 0:slen]),
            inc="qxdbl", n=16)
        P.rdone(tk, "xdbl")
        P.wait(SC, tk)
        tok_xd = P.mark(SC, "cxd")
        yield

        for m in range(8):
            bank = 1 + (m % 2)
            P.rd(TE, "dtlow")
            P.wr(TE, f"pb{bank}")
            tt = P.run(TE, lambda E, m=m, bank=bank, slen=slen: E.matmul(
                pbk[bank][:, 0:slen], w_dt[:, m * 128:(m + 1) * 128],
                dtlow[:, 0:slen], start=True, stop=True), inc="cte")
            P.rdone(tt, "dtlow")
            P.wdone(tt, f"pb{bank}")
            P.rd(SE, f"pb{bank}", "c_inv2")
            P.wr(SE, "dtmp0")
            ta = P.run(SE, lambda E, m=m, bank=bank, slen=slen: E.activation(
                dtmp[0][:, 0:slen], pbk[bank][:, 0:slen], AF.Abs,
                bias=c_inv2[:, m:m + 1]), inc="cse")
            P.wdone(ta, "dtmp0")
            P.wr(SE, "rvt")
            tr = P.run(SE, lambda E, m=m, bank=bank, slen=slen: E.activation(
                rvt[:, 0:slen], pbk[bank][:, 0:slen], AF.Relu,
                bias=c_inv2[:, m:m + 1]), inc="cse")
            P.rdone(tr, f"pb{bank}")
            P.wdone(tr, "rvt")
            P.rd(SE, "dtmp0")
            tx = P.run(SE, lambda E, slen=slen: E.activation(
                dtmp[0][:, 0:slen], dtmp[0][:, 0:slen], AF.Exp, scale=-1.0),
                inc="cse")
            P.wdone(tx, "dtmp0")
            P.rd(SE, "dtmp0")
            tl = P.run(SE, lambda E, slen=slen: E.activation(
                dtmp[0][:, 0:slen], dtmp[0][:, 0:slen], AF.Ln, bias=1.0),
                inc="cse")
            P.wdone(tl, "dtmp0")
            P.rd(VE, "dtmp0", "rvt")
            P.wr(VE, f"xfm{m}")
            tv = P.run(VE, lambda E, m=m, slen=slen: E.tensor_add(
                xfm[:, m * SW:m * SW + slen], rvt[:, 0:slen],
                dtmp[0][:, 0:slen]), inc="cve")
            P.rdone(tv, "dtmp0")
            P.rdone(tv, "rvt")
            P.wdone(tv, f"xfm{m}")
            if m % 4 == 3:
                yield

        for dt in range(8):
            P.rd(VE, f"ub{dt}")
            P.wr(VE, f"yfm{dt}")
            tv = P.run(VE, lambda E, dt=dt, slen=slen: E.tensor_scalar_mul(
                yfm[:, dt * SST:dt * SST + slen],
                ub[:, dt * SST:dt * SST + slen], c_D[:, dt:dt + 1]),
                inc="cve")
            P.rdone(tv, f"ub{dt}")
            P.wdone(tv, f"yfm{dt}")
            P.rd(VE, f"xfm{dt}", f"ub{dt}")
            P.wr(VE, f"ub{dt}")
            tv = P.run(VE, lambda E, dt=dt, slen=slen: E.tensor_mul(
                ub[:, dt * SST:dt * SST + slen],
                xfm[:, dt * SW:dt * SW + slen],
                ub[:, dt * SST:dt * SST + slen]), inc="cve")
            P.rdone(tv, f"xfm{dt}")
            P.wdone(tv, f"ub{dt}")

        for n in range(NST):
            bslot = n % 2
            bc = bc_r[bslot]
            P.wait(PL, tok_xd)
            P.wr(PL, f"bc{bslot}")
            tb1 = P.run(PL, lambda E, n=n, t0=t0, slen=slen, bc=bc: E.dma_start(
                out=bc[:, 0:slen],
                in_=xdb_s[n:n + 1, t0:t0 + slen].broadcast_to([128, slen])),
                inc=f"qbcB{bslot}", n=16)
            tb2 = P.run(PL, lambda E, n=n, t0=t0, slen=slen, bc=bc: E.dma_start(
                out=bc[:, 512:512 + slen],
                in_=xdb_s[16 + n:17 + n, t0:t0 + slen].broadcast_to([128, slen])),
                inc=f"qbcC{bslot}", n=16)
            P.wdone(tb2, f"bc{bslot}")
            P.raw[f"bcB{bslot}"] = tb1
            for j in range(4):
                i2 = n * 4 + j
                asl, hsl, tsl = i2 % 2, i2 % 3, i2 % 2
                d0, d1 = 2 * j, 2 * j + 1
                dpk = (f"ub{d0}", f"ub{d1}")
                dv = xfm[:, d0 * SW:(d1 + 1) * SW].rearrange(
                    "p (a m) -> p a m", a=2)[:, :, 0:slen]
                P.rd(SE, f"xfm{d0}", f"xfm{d1}")
                P.wr(SE, f"a{asl}")
                av = a_r[asl][:, 0:2 * SST].rearrange(
                    "p (a m) -> p a m", a=2)[:, :, 0:slen]
                ta = P.run(SE, lambda E, n=n, dv=dv, av=av: E.activation(
                    av, dv, AF.Exp, scale=-(n + 1.0)), inc="cse")
                P.wdone(ta, f"a{asl}")
                duv = ub[:, d0 * SST:d0 * SST + 2 * SST].rearrange(
                    "p (a m) -> p a m", a=2)[:, :, 0:slen]
                bbv = bc[:, 0:slen].rearrange("p (a m) -> p a m", a=1
                                              ).broadcast_to([128, 2, slen])
                bv = b_r[asl][:, 0:2 * SST].rearrange(
                    "p (a m) -> p a m", a=2)[:, :, 0:slen]
                P.rd(VE, *dpk, f"bc{bslot}", f"bcB{bslot}")
                P.wr(VE, f"b{asl}")
                tbv = P.run(VE, lambda E, duv=duv, bbv=bbv, bv=bv:
                            E.tensor_tensor(bv, duv, bbv, ALU.mult), inc="cve")
                P.rdone(tbv, *dpk)
                P.wdone(tbv, f"b{asl}")
                for q in range(2):
                    i = n * 8 + d0 + q
                    P.rd(VE, f"a{asl}", f"b{asl}")
                    if s > 0:
                        P.rd(VE, f"car{s % 2}_{i}")
                    P.wr(VE, f"h{hsl}")
                    init = 0.0 if s == 0 else carry[s % 2][:, i:i + 1]
                    th = P.run(VE, lambda E, asl=asl, hsl=hsl, q=q, init=init,
                               slen=slen: E.tensor_tensor_scan(
                        h_r[hsl][:, q * SST:q * SST + slen],
                        a_r[asl][:, q * SST:q * SST + slen],
                        b_r[asl][:, q * SST:q * SST + slen],
                        init, ALU.mult, ALU.add), inc="cve")
                    P.rdone(th, f"a{asl}")
                    P.rdone(th, f"b{asl}")
                    if s > 0:
                        P.rdone(th, f"car{s % 2}_{i}")
                    P.wdone(th, f"h{hsl}")
                if s + 1 < ns:
                    i = n * 8 + d0
                    hv = h_r[hsl][:, 0:2 * SST].rearrange(
                        "p (a m) -> p a m", a=2)[:, :, slen - 1:slen]
                    P.rd(VE, f"h{hsl}")
                    P.wr(VE, f"car{(s + 1) % 2}_{i}", f"car{(s + 1) % 2}_{i + 1}")
                    tc = P.run(VE, lambda E, hv=hv, i=i, s=s: E.tensor_copy(
                        carry[(s + 1) % 2][:, i:i + 2], hv), inc="cve")
                    P.wdone(tc, f"car{(s + 1) % 2}_{i}",
                            f"car{(s + 1) % 2}_{i + 1}")
                cbv = bc[:, 512:512 + slen].rearrange(
                    "p (a m) -> p a m", a=1).broadcast_to([128, 2, slen])
                hvv = h_r[hsl][:, 0:2 * SST].rearrange(
                    "p (a m) -> p a m", a=2)[:, :, 0:slen]
                tvv = t_r[tsl][:, 0:2 * SST].rearrange(
                    "p (a m) -> p a m", a=2)[:, :, 0:slen]
                P.rd(VE, f"h{hsl}", f"bc{bslot}")
                P.wr(VE, f"t{tsl}")
                tp = P.run(VE, lambda E, hvv=hvv, cbv=cbv, tvv=tvv:
                           E.tensor_tensor(tvv, hvv, cbv, ALU.mult), inc="cve")
                P.rdone(tp, f"h{hsl}")
                P.rdone(tp, f"bc{bslot}")
                P.wdone(tp, f"t{tsl}")
                yk = (f"yfm{d0}", f"yfm{d1}")
                P.rd(PL, f"t{tsl}", *yk)
                P.wr(PL, *yk)
                ty = P.run(PL, lambda E, d0=d0, tsl=tsl: E.tensor_tensor(
                    yfm[:, d0 * SST:(d0 + 2) * SST],
                    yfm[:, d0 * SST:(d0 + 2) * SST],
                    t_r[tsl][:, 0:2 * SST], ALU.add), inc="cpl")
                P.rdone(ty, f"t{tsl}")
                P.wdone(ty, *yk)
                yield

        for dt in range(8):
            P.rd(SE, f"yfm{dt}")
            P.wr(SE, f"xsb{dt}")
            te = P.run(SE, lambda E, dt=dt, slen=slen: E.copy(
                xsb[:, dt * SW:dt * SW + slen],
                yfm[:, dt * SST:dt * SST + slen]), inc="cse")
            P.rdone(te, f"yfm{dt}")
            P.wdone(te, f"xsb{dt}")
        yield

        for _ in range(24):
            yield
        for ki, k in enumerate(list(range(8, 16)) + list(range(8))):
            kb = ki % 2
            dma(SY, woutb[kb][:, :], wout_s[k * 128:(k + 1) * 128, :],
                f"qwout{kb}", key=f"woutb{kb}")
            P.rd(TE, f"woutb{kb}")
            if k < 8:
                P.rd(TE, f"xsb{k}")
            else:
                P.rd(TE, f"zcb{k - 8}")
            for m in range(8):
                if ki == 0:
                    P.wr(TE, f"pb{m}")
                rhs = (xsb[:, k * SW:k * SW + slen] if k < 8
                       else zcb[:, (k - 8) * SST:(k - 8) * SST + slen])
                tt = P.run(TE, lambda E, k=k, m=m, kb=kb, rhs=rhs, ki=ki:
                           E.matmul(
                    pbk[m][:, 0:slen], woutb[kb][:, m * 128:(m + 1) * 128],
                    rhs, start=(ki == 0), stop=(ki == 15)),
                    inc="cte" if m == 7 else None)
                if ki == 15:
                    P.wdone((("cte", P.cnt["cte"] + 1)
                             if m < 7 else tt), f"pb{m}")
            P.rdone(tt, f"woutb{kb}")
            if k < 8:
                P.rdone(tt, f"xsb{k}")
            else:
                P.rdone(tt, f"zcb{k - 8}")
        for m in range(8):
            P.rd(SE, f"pb{m}")
            P.wr(SE, f"zb{m}")
            te = P.run(SE, lambda E, m=m, slen=slen: E.activation(
                zb[:, m * SW:m * SW + slen], pbk[m][:, 0:slen], AF.Identity,
                bias=c_bout[:, m:m + 1]), inc="cse")
            P.rdone(te, f"pb{m}")
            P.wdone(te, f"zb{m}")

        ngo = (slen + 127) // 128
        for g in range(ngo):
            gw = min(128, slen - g * 128)
            aslot = g % 2
            akey = f"attm{aslot}"
            P.wr(TE, "pb7")
            for dt in range(8):
                P.rd(TE, f"zb{dt}")
                tt = P.run(TE, lambda E, dt=dt, g=g, gw=gw: E.matmul(
                    pbf[7][0:gw, dt * 128:dt * 128 + 128],
                    zb[:, dt * SW + g * 128:dt * SW + g * 128 + gw],
                    c_idbf[:, :], start=(dt == 0), stop=(dt == 7),
                    is_transpose=True), inc="cte")
                P.rdone(tt, f"zb{dt}")
            P.wdone(tt, "pb7")
            P.rd(VE, "pb7")
            P.wr(VE, akey)
            tv = P.run(VE, lambda E, aslot=aslot, gw=gw: E.tensor_copy(
                attm[aslot][0:gw, :], pbf[7][0:gw, :]), inc="cve")
            P.rdone(tv, "pb7")
            P.wdone(tv, akey)
            P.rd(SC, akey)
            tk = P.run(SC, lambda E, t0=t0, g=g, gw=gw, aslot=aslot:
                       E.dma_start(
                out=att_s[t0 + g * 128:t0 + g * 128 + gw, :],
                in_=attm[aslot][0:gw, :]), inc=f"qscr{aslot}", n=16)
            P.rdone(tk, akey)
            yield
        for sl in range(2):
            if f"qscr{sl}" in P.cnt:
                P.wait(SC, (f"qscr{sl}", P.cnt[f"qscr{sl}"]))
        scr_out.append(P.mark(SC, "cscr"))

    def emit_round(r, scr_list):
        for g in range(4):
            gs = g % 2
            need = (2 * r + 1) * TB + (g + 1) * 128
            sidx = min((need + SST - 1) // SST - 1, ns - 1)
            P.wait(PL, scr_list[sidx])
            P.wait(PL, tok_const)
            P.wr(PL, "attg0")
            tg = P.run(PL, lambda E, r=r, g=g: E.indirect_dma_start(
                attg[0][:, :], None, att_s[:, :],
                IndirectOffsetOnAxis(c_idx[:, r * 4 + g:r * 4 + g + 1], 0)),
                inc="qgat0", n=16)
            P.wdone(tg, "attg0")
            P.wr(SC, "xres0")
            tr = P.run(SC, lambda E, r=r, g=g: E.dma_start(
                out=xresb[0][:, :],
                in_=xres_d[r * TB + g * 128:r * TB + g * 128 + 128, :]),
                inc=f"qres{gs}", n=16)
            P.wdone(tr, "xres0")
            P.rd(VE, "attg0", "xres0")
            P.wr(VE, "xres0")
            tv = P.run(VE, lambda E: E.tensor_add(
                xresb[0][:, :], xresb[0][:, :], attg[0][:, :]), inc="cve")
            P.rdone(tv, "attg0")
            P.wdone(tv, "xres0")
            P.rd(VE, "xres0")
            t1 = P.run(VE, lambda E: E.reduce_sum(
                rs_t[:, 0:1], xresb[0][:, :], axis=AX), inc="cve")
            P.rd(SE, "xres0")
            P.wr(SE, OKEY[gs])
            t2 = P.run(SE, lambda E, gs=gs: E.activation(
                outst[gs][:, :], xresb[0][:, :], AF.Square), inc="cse")
            P.wdone(t2, OKEY[gs])
            P.rd(VE, OKEY[gs])
            t3 = P.run(VE, lambda E, gs=gs: E.reduce_sum(
                rs_t[:, 1:2], outst[gs][:, :], axis=AX), inc="cve")
            P.rdone(t3, OKEY[gs])
            P.wait(VE, t1)
            t4 = P.run(VE, lambda E: E.tensor_single_scalar(
                rs_t[:, 2:3], rs_t[:, 0:1], 1.0 / D, ALU.mult), inc="cve")
            P.wait(VE, t3)
            t5 = P.run(VE, lambda E: E.tensor_single_scalar(
                rs_t[:, 3:4], rs_t[:, 1:2], 1.0 / D, ALU.mult), inc="cve")
            P.wait(VE, t4)
            t6 = P.run(VE, lambda E: E.tensor_mul(
                rs_t[:, 4:5], rs_t[:, 2:3], rs_t[:, 2:3]), inc="cve")
            P.wait(VE, t5)
            P.wait(VE, t6)
            t7 = P.run(VE, lambda E: E.tensor_sub(
                rs_t[:, 5:6], rs_t[:, 3:4], rs_t[:, 4:5]), inc="cve")
            P.wait(VE, t7)
            t7b = P.run(VE, lambda E: E.tensor_single_scalar(
                rs_t[:, 5:6], rs_t[:, 5:6], 1e-5, ALU.add), inc="cve")
            P.wait(SE, t7b)
            t8 = P.run(SE, lambda E: E.activation(
                rs_t[:, 6:7], rs_t[:, 5:6], AF.Sqrt), inc="cse")
            P.wait(VE, t8)
            t9 = P.run(VE, lambda E: E.reciprocal(rs_t[:, 7:8], rs_t[:, 6:7]),
                       inc="cve")
            P.wait(VE, t9)
            ta = P.run(VE, lambda E: E.tensor_single_scalar(
                rs_t[:, 8:9], rs_t[:, 7:8], 2.0, ALU.mult), inc="cve")
            P.wait(VE, ta)
            P.rd(VE, "xres0")
            P.wr(VE, f"yacc{g}")
            tb3 = P.run(VE, lambda E, g=g: E.tensor_scalar(
                yacc[:, g * D:(g + 1) * D], xresb[0][:, :], rs_t[:, 2:3],
                rs_t[:, 8:9], ALU.subtract, ALU.mult), inc="cve")
            P.rdone(tb3, "xres0")
            P.wdone(tb3, f"yacc{g}")
            yield

        for dt in range(8):
            P.wr(TE, "pb7")
            for g in range(4):
                P.rd(TE, f"yacc{g}")
                tt = P.run(TE, lambda E, dt=dt, g=g: E.matmul(
                    pbf[7][:, g * 128:(g + 1) * 128],
                    yacc[:, g * D + dt * 128:g * D + dt * 128 + 128],
                    c_idbf[:, :], start=(g == 0), stop=(g == 3),
                    is_transpose=True), inc="cte")
                P.rdone(tt, f"yacc{g}")
            P.wdone(tt, "pb7")
            P.rd(SE, "pb7")
            P.wr(SE, f"xtf{dt}")
            te = P.run(SE, lambda E, dt=dt: E.copy(
                xtf[:, dt * TB:(dt + 1) * TB], pbf[7][:, 0:TB]), inc="cse")
            P.rdone(te, "pb7")
            P.wdone(te, f"xtf{dt}")
            if dt % 4 == 3:
                yield

        for k in range(8):
            kb = k % 3
            dma(SC, w2b[kb][:, :], rw1_s[k * 128:(k + 1) * 128, :],
                f"qw2{kb}", key=f"w2b{kb}", waits=(tok_stgS,))
            P.rd(TE, f"w2b{kb}", f"xtf{k}")
            for m in range(8):
                if k == 0:
                    P.wr(TE, f"pb{m}")
                tt = P.run(TE, lambda E, k=k, m=m, kb=kb: E.matmul(
                    pbk[m][:, 0:TB], w2b[kb][:, m * 128:(m + 1) * 128],
                    xtf[:, k * TB:(k + 1) * TB],
                    start=(k == 0), stop=(k == 7)),
                    inc="cte" if m == 7 else None)
                if k == 7:
                    P.wdone((("cte", P.cnt["cte"] + 1)
                             if m < 7 else tt), f"pb{m}")
            P.rdone(tt, f"w2b{kb}")
            P.rdone(tt, f"xtf{k}")
        for m in range(8):
            P.rd(SE, f"pb{m}")
            P.wr(SE, f"h{m}")
            te = P.run(SE, lambda E, m=m: E.activation(
                hbuf[:, m * TB:(m + 1) * TB], pbk[m][:, 0:TB], AF.Silu,
                bias=c_rb1[:, m:m + 1], scale=0.5), inc="cse")
            P.rdone(te, f"pb{m}")
            P.wdone(te, f"h{m}")
        P.wait(TE, tok_small)
        for k in range(8):
            P.rd(TE, f"h{k}")
            if k == 0:
                P.wr(TE, "pb0")
            tt = P.run(TE, lambda E, k=k: E.matmul(
                pbk[0][0:NE, 0:TB], w_rw2[:, k * NE:(k + 1) * NE],
                hbuf[:, k * TB:(k + 1) * TB], start=(k == 0), stop=(k == 7)),
                inc="cte")
            P.rdone(tt, f"h{k}")
        P.wdone(tt, "pb0")
        P.rd(VE, "pb0")
        P.wr(VE, "lgf")
        tv = P.run(VE, lambda E: E.tensor_scalar_add(
            lgf[:, :], pbk[0][0:NE, 0:TB], c_rb2[:, 0:1]), inc="cve")
        P.rdone(tv, "pb0")
        P.wdone(tv, "lgf")
        P.wr(TE, "pb1")
        P.rd(TE, "lgf")
        for g in range(4):
            tt = P.run(TE, lambda E, g=g: E.matmul(
                pbk[1][:, g * 8:g * 8 + 8], lgf[:, g * 128:(g + 1) * 128],
                c_id32[0:8, 0:8], start=(g == 0), stop=(g == 3),
                is_transpose=True), inc="cte" if g == 3 else None)
        P.rdone(tt, "lgf")
        P.wdone(tt, "pb1")
        P.rd(VE, "pb1")
        P.wr(VE, "lgt")
        tv = P.run(VE, lambda E: E.tensor_copy(lgt[:, :], pbk[1][:, 0:32]),
                   inc="cve")
        P.rdone(tv, "pb1")
        P.wdone(tv, "lgt")
        for g in range(4):
            lg = lgt[:, g * 8:(g + 1) * 8]
            P.rd(VE, "lgt")
            t1 = P.run(VE, lambda E, lg=lg: E.reduce_max(
                rs_t[:, 9:10], lg, axis=AX), inc="cve")
            P.wait(VE, t1)
            t2 = P.run(VE, lambda E, lg=lg: E.tensor_single_scalar(
                eqt[:, :], lg, rs_t[:, 9:10], ALU.is_equal), inc="cve")
            P.wait(VE, t2)
            t3 = P.run(VE, lambda E, lg=lg: E.scalar_tensor_tensor(
                ewt[:, :], eqt[:, :], -1e30, lg, ALU.mult, ALU.add),
                inc="cve")
            P.wait(VE, t3)
            t4 = P.run(VE, lambda E: E.reduce_max(
                rs_t[:, 10:11], ewt[:, :], axis=AX), inc="cve")
            t5 = P.run(VE, lambda E: E.tensor_single_scalar(
                rs_t[:, 11:12], rs_t[:, 9:10], -1.0, ALU.mult), inc="cve")
            P.wait(SE, t5)
            t6 = P.run(SE, lambda E, lg=lg: E.activation(
                ett[:, :], lg, AF.Exp, bias=rs_t[:, 11:12]), inc="cse")
            P.wait(VE, t4)
            t7 = P.run(VE, lambda E, lg=lg: E.tensor_single_scalar(
                eqt[:, :], lg, rs_t[:, 10:11], ALU.is_ge), inc="cve")
            P.wait(VE, t6)
            P.wait(VE, t7)
            t8 = P.run(VE, lambda E: E.tensor_mul(
                ewt[:, :], ett[:, :], eqt[:, :]), inc="cve")
            P.wait(VE, t8)
            t9 = P.run(VE, lambda E: E.reduce_sum(
                rs_t[:, 12:13], ewt[:, :], axis=AX), inc="cve")
            P.wait(VE, t9)
            ta = P.run(VE, lambda E: E.reciprocal(
                rs_t[:, 13:14], rs_t[:, 12:13]), inc="cve")
            P.wait(VE, ta)
            P.wr(VE, f"wtm{g}")
            tb3 = P.run(VE, lambda E, g=g: E.tensor_scalar_mul(
                wtm[:, g * 8:(g + 1) * 8], ewt[:, :], rs_t[:, 13:14]),
                inc="cve")
            P.wdone(tb3, f"wtm{g}")
        P.wr(TE, "pb2")
        for g in range(4):
            P.rd(TE, f"wtm{g}")
            tt = P.run(TE, lambda E, g=g: E.matmul(
                pbk[2][0:8, g * 128:(g + 1) * 128], wtm[:, g * 8:(g + 1) * 8],
                c_id32[:, :], start=(g == 0), stop=(g == 3),
                is_transpose=True), inc="cte" if g == 3 else None)
            P.rdone(tt, f"wtm{g}")
        P.wdone(tt, "pb2")
        P.rd(SE, "pb2")
        P.wr(SE, "wfmb")
        te = P.run(SE, lambda E: E.copy(wfmb[:, :], pbk[2][0:8, 0:TB]),
                   inc="cse")
        P.rdone(te, "pb2")
        P.wdone(te, "wfmb")
        for g in range(4):
            for hh in range(2):
                bank = 3 + hh
                P.rd(TE, "wfmb")
                P.wr(TE, f"pb{bank}")
                tt = P.run(TE, lambda E, g=g, hh=hh, bank=bank: E.matmul(
                    pbk[bank][:, 0:TB], wfmb[:, g * 128:(g + 1) * 128],
                    w_be2[:, hh * TB:(hh + 1) * TB], start=True, stop=True),
                    inc="cte")
                P.rdone(tt, "wfmb")
                P.wdone(tt, f"pb{bank}")
                P.rd(VE, f"pb{bank}", f"yacc{g}")
                P.wr(VE, f"yacc{g}")
                tv = P.run(VE, lambda E, g=g, hh=hh, bank=bank: E.tensor_add(
                    yacc[:, g * D + hh * TB:g * D + (hh + 1) * TB],
                    yacc[:, g * D + hh * TB:g * D + (hh + 1) * TB],
                    pbk[bank][:, 0:TB]), inc="cve")
                P.rdone(tv, f"pb{bank}")
                P.wdone(tv, f"yacc{g}")
        yield

        for e in range(NE):
            for hg in range(16):
                mb = hg % 4
                src1 = we1_s[e][:, hg * 256:(hg + 1) * 256].rearrange(
                    "(k p) m -> p k m", p=128)
                for half in range(2):
                    dst = w1b[mb][:, half * 1024:(half + 1) * 1024].rearrange(
                        "p (k m) -> p k m", k=4)
                    dma(SC, dst, src1[:, half * 4:(half + 1) * 4, :],
                        f"qw1{mb}", key=f"w1b{mb}",
                        waits=(("qstgE", 16 * 2 * NE),))
                P.rd(TE, f"w1b{mb}")
                for mt in range(2):
                    bank = 5 + (mt % 2)
                    P.wr(TE, f"pb{bank}")
                    for k in range(8):
                        P.rd(TE, f"xtf{k}")
                        tt = P.run(TE, lambda E, mb=mb, mt=mt, k=k, bank=bank:
                                   E.matmul(
                            pbk[bank][:, 0:TB],
                            w1b[mb][:, k * 256 + mt * 128:k * 256 + mt * 128 + 128],
                            xtf[:, k * TB:(k + 1) * TB],
                            start=(k == 0), stop=(k == 7)),
                            inc="cte" if k == 7 else None)
                    P.wdone(tt, f"pb{bank}")
                    hm = hg * 2 + mt
                    P.rd(SE, f"pb{bank}")
                    P.wr(SE, f"h{hm}")
                    te = P.run(SE, lambda E, e=e, hm=hm, bank=bank:
                               E.activation(
                        hbuf[:, hm * TB:(hm + 1) * TB], pbk[bank][:, 0:TB],
                        AF.Silu, bias=c_be1[:, e * 32 + hm:e * 32 + hm + 1],
                        scale=0.5), inc="cse")
                    P.rdone(te, f"pb{bank}")
                    P.wdone(te, f"h{hm}")
                    yield
                P.rdone(tt, f"w1b{mb}")
            for tt4 in range(4):
                for hh in range(2):
                    P.wr(TE, f"pb{tt4 * 2 + hh}")
            for k in range(32):
                kb = k % 3
                dma(SC, w2b[kb][:, :], we2_s[e][k * 128:(k + 1) * 128, :],
                    f"qw2{kb}", key=f"w2b{kb}")
                P.rd(TE, f"w2b{kb}", f"h{k}")
                for tt4 in range(4):
                    for hh in range(2):
                        tt = P.run(TE, lambda E, k=k, kb=kb, tt4=tt4, hh=hh:
                                   E.matmul(
                            pbk[tt4 * 2 + hh][:, 0:TB],
                            hbuf[:, k * TB + tt4 * 128:k * TB + tt4 * 128 + 128],
                            w2b[kb][:, hh * TB:(hh + 1) * TB],
                            start=(k == 0), stop=(k == 31)),
                            inc="cte" if (tt4 == 3 and hh == 1) else None)
                P.rdone(tt, f"w2b{kb}")
                P.rdone(tt, f"h{k}")
            tlast = tt
            for tt4 in range(4):
                for hh in range(2):
                    bank = tt4 * 2 + hh
                    P.wait(VE, tlast)
                    P.rd(VE, f"pb{bank}", f"yacc{tt4}", f"wtm{tt4}")
                    P.wr(VE, f"yacc{tt4}")
                    tv = P.run(VE, lambda E, e=e, tt4=tt4, hh=hh, bank=bank:
                               E.scalar_tensor_tensor(
                        yacc[:, tt4 * D + hh * TB:tt4 * D + (hh + 1) * TB],
                        pbk[bank][:, 0:TB],
                        wtm[:, tt4 * 8 + e:tt4 * 8 + e + 1],
                        yacc[:, tt4 * D + hh * TB:tt4 * D + (hh + 1) * TB],
                        ALU.mult, ALU.add), inc="cve")
                    P.rdone(tv, f"pb{bank}")
                    P.wdone(tv, f"yacc{tt4}")
            yield

        for g in range(4):
            gs = g % 2
            P.rd(VE, f"yacc{g}")
            t1 = P.run(VE, lambda E, g=g: E.reduce_sum(
                rs_t[:, 0:1], yacc[:, g * D:(g + 1) * D], axis=AX), inc="cve")
            P.rd(SE, f"yacc{g}")
            P.wr(SE, OKEY[gs])
            t2 = P.run(SE, lambda E, g=g, gs=gs: E.activation(
                outst[gs][:, :], yacc[:, g * D:(g + 1) * D], AF.Square),
                inc="cse")
            P.wdone(t2, OKEY[gs])
            P.rd(VE, OKEY[gs])
            t3 = P.run(VE, lambda E, gs=gs: E.reduce_sum(
                rs_t[:, 1:2], outst[gs][:, :], axis=AX), inc="cve")
            P.rdone(t3, OKEY[gs])
            P.wait(VE, t1)
            t4 = P.run(VE, lambda E: E.tensor_single_scalar(
                rs_t[:, 2:3], rs_t[:, 0:1], 1.0 / D, ALU.mult), inc="cve")
            P.wait(VE, t3)
            t5 = P.run(VE, lambda E: E.tensor_single_scalar(
                rs_t[:, 3:4], rs_t[:, 1:2], 1.0 / D, ALU.mult), inc="cve")
            P.wait(VE, t4)
            t6 = P.run(VE, lambda E: E.tensor_mul(
                rs_t[:, 4:5], rs_t[:, 2:3], rs_t[:, 2:3]), inc="cve")
            P.wait(VE, t5)
            P.wait(VE, t6)
            t7 = P.run(VE, lambda E: E.tensor_sub(
                rs_t[:, 5:6], rs_t[:, 3:4], rs_t[:, 4:5]), inc="cve")
            P.wait(VE, t7)
            t7b = P.run(VE, lambda E: E.tensor_single_scalar(
                rs_t[:, 5:6], rs_t[:, 5:6], 1e-5, ALU.add), inc="cve")
            P.wait(SE, t7b)
            t8 = P.run(SE, lambda E: E.activation(
                rs_t[:, 6:7], rs_t[:, 5:6], AF.Sqrt), inc="cse")
            P.wait(VE, t8)
            t9 = P.run(VE, lambda E: E.reciprocal(rs_t[:, 7:8], rs_t[:, 6:7]),
                       inc="cve")
            P.wait(VE, t9)
            P.rd(VE, f"yacc{g}")
            P.wr(VE, OKEY[gs])
            ta = P.run(VE, lambda E, g=g, gs=gs: E.tensor_scalar(
                outst[gs][:, :], yacc[:, g * D:(g + 1) * D], rs_t[:, 2:3],
                rs_t[:, 7:8], ALU.subtract, ALU.mult), inc="cve")
            P.rdone(ta, f"yacc{g}")
            P.wdone(ta, OKEY[gs])
            P.rd(SC, OKEY[gs])
            tk = P.run(SC, lambda E, r=r, g=g, gs=gs: E.dma_start(
                out=out_d[r * TB + g * 128:r * TB + g * 128 + 128, :],
                in_=outst[gs][:, :]), inc=f"qout{gs}", n=16)
            P.rdone(tk, OKEY[gs])
            yield

    def stripe_for_round(r):
        need = (2 * r + 2) * TB
        s = (need + SST - 1) // SST - 1
        return min(s, ns - 1)

    scr_toks = []
    stage_state = [0]

    def flush_stages(n):
        while stage_state[0] < NE and n > 0:
            a1, a2, a3, a4 = stage_chunks[stage_state[0]]
            dma(PL, a1[:, :], a2[:, :], "qstgE")
            dma(PL, a3[:, :], a4[:, :], "qstgE")
            stage_state[0] += 1
            n -= 1

    s_i, r_i = 0, 0
    sgen, rgen = None, None
    while s_i < ns or r_i < nrounds or sgen is not None or rgen is not None:
        if sgen is None and s_i < ns:
            flush_stages(2)
            sgen = emit_stripe(s_i, stripes[s_i][0], stripes[s_i][1], scr_toks)
        if rgen is None and r_i < nrounds and                 len(scr_toks) > stripe_for_round(r_i):
            flush_stages(NE)
            rgen = emit_round(r_i, scr_toks)
        for _ in range(2):
            if rgen is not None:
                try:
                    next(rgen)
                except StopIteration:
                    rgen = None
                    r_i += 1
        if sgen is not None:
            try:
                next(sgen)
            except StopIteration:
                sgen = None
                s_i += 1
        if sgen is None and rgen is None and s_i >= ns and r_i < nrounds and                 len(scr_toks) <= stripe_for_round(r_i):
            break

    for sl in range(2):
        if f"qout{sl}" in P.cnt:
            P.wait(SY, (f"qout{sl}", P.cnt[f"qout{sl}"]))

    sems = {}
    for name in P.cnt:
        sems[name] = es.enter_context(nc.semaphore(name))
    blk = es.enter_context(nc.Block())
    with es:
        def runner(eng_name):
            def f(E):
                for op in P.ops[eng_name]:
                    if op[0] == "w":
                        E.wait_ge(sems[op[1]], op[2])
                    elif op[0] == "i":
                        E.sem_inc(sems[op[1]], op[2])
                    else:
                        _, fn, incn, n = op
                        ins = fn(E)
                        if incn is not None:
                            ins.then_inc(sems[incn], n)
            return f

        blk.sync(runner("sync"))
        blk.scalar(runner("scalar"))
        blk.vector(runner("vector"))
        blk.tensor(runner("tensor"))
        blk.gpsimd(runner("gpsimd"))
    return nc


def _inmaps(inputs, L):
    WIN = L // 2
    nblocks = L // TB
    x4 = np.ascontiguousarray(inputs["L0_attn_x"][:, :L])
    cw = lambda w: np.ascontiguousarray(
        np.transpose(w[:, 0, :].reshape(4, 8, 128), (2, 1, 0)).reshape(128, 32))
    vec8 = lambda v: np.ascontiguousarray(v.reshape(8, 128).T)
    be1c = np.ascontiguousarray(
        np.transpose(inputs["L0_ffn_be1"].reshape(NE, 32, 128), (2, 0, 1)
                     ).reshape(128, NE * 32))
    shared = dict(
        W_in=inputs["L0_attn_W_in"],
        cwx=cw(inputs["L0_attn_conv_x_w"]), cbx=vec8(inputs["L0_attn_conv_x_b"]),
        cwz=cw(inputs["L0_attn_conv_z_w"]), cbz=vec8(inputs["L0_attn_conv_z_b"]),
        W_xdbl=inputs["L0_attn_W_xdbl"], W_dt=inputs["L0_attn_W_dt"],
        invdt=vec8(inputs["L0_attn_inv_dt"]), Dv=vec8(inputs["L0_attn_D"]),
        W_out=inputs["L0_attn_W_out"], bout=vec8(inputs["L0_attn_b_out"]),
        rw1=inputs["L0_ffn_rw1"], rb1=vec8(inputs["L0_ffn_rb1"]),
        rw2=inputs["L0_ffn_rw2"],
        rb2=np.ascontiguousarray(inputs["L0_ffn_rb2"].reshape(NE, 1)),
        we1=inputs["L0_ffn_we1"], be1c=be1c,
        we2=inputs["L0_ffn_we2"], be2=inputs["L0_ffn_be2"],
        ident=np.eye(128, dtype=np.float32),
    )
    shared = {k: np.ascontiguousarray(v, dtype=np.float32)
              for k, v in shared.items()}
    maps = []
    for c in range(8):
        p, m = c // 2, c % 2
        blocks = np.arange(m, nblocks, 2)
        idx = np.ascontiguousarray(
            (blocks[:, None, None] * TB + np.arange(4)[None, :, None] * 128
             + np.arange(128)[None, None, :]).reshape(-1, 128).T.astype(np.int32))
        xres = np.ascontiguousarray(
            x4[p].reshape(nblocks, TB, D)[m::2].reshape(WIN, D))
        mp = dict(shared)
        mp["x"] = np.ascontiguousarray(x4[p])
        mp["xres"] = xres
        mp["idx"] = idx
        maps.append(mp)
    return maps


def kernel(**inputs):
    L = inputs["L0_attn_x"].shape[1]
    WIN = L // 2
    nblocks = L // TB
    nc = build(L)
    maps = _inmaps(inputs, L)
    res = run_bass_kernel_spmd(nc, maps, core_ids=list(range(8)))
    out = np.empty((4, L, D), dtype=np.float32)
    for c in range(8):
        p, m = c // 2, c % 2
        o = np.asarray(res.results[c]["out"], dtype=np.float32)
        out[p].reshape(nblocks, TB, D)[m::2] = o.reshape(nblocks // 2, TB, D)
    return out


# version 11
# speedup vs baseline: 1.1933x; 1.1933x over previous
import contextlib
import numpy as np

import concourse.bass as bass
import concourse.mybir as mybir
from concourse.bass_utils import run_bass_kernel_spmd
from concourse.bass_core_rust import IndirectOffsetOnAxis

F32 = mybir.dt.float32
BF16 = mybir.dt.bfloat16
I32 = mybir.dt.int32
AF = mybir.ActivationFunctionType
ALU = mybir.AluOpType
AX = mybir.AxisListType.X

D, DI, NST, DTR, NE, HE = 1024, 2048, 16, 64, 8, 4096
SW = 512
SST = SW - 3
TB = 512
ENGS = ("sync", "scalar", "vector", "tensor", "gpsimd")


class Pb:

    def __init__(self):
        self.ops = {e: [] for e in ENGS}
        self.cnt = {}
        self.hw = {}
        self.raw = {}
        self.war = {}

    def _sem(self, name):
        self.cnt.setdefault(name, 0)

    def wait(self, eng, tok):
        if tok is None:
            return
        s, v = tok
        if self.hw.get((eng, s), 0) >= v:
            return
        self.hw[(eng, s)] = v
        self.ops[eng].append(("w", s, v))

    def run(self, eng, fn, inc=None, n=1):
        tok = None
        if inc is not None:
            self._sem(inc)
            self.cnt[inc] += n
            tok = (inc, self.cnt[inc])
        self.ops[eng].append(("r", fn, inc, n))
        return tok

    def mark(self, eng, sem, n=1):
        self._sem(sem)
        self.cnt[sem] += n
        self.ops[eng].append(("i", sem, n))
        return (sem, self.cnt[sem])

    def rd(self, eng, *keys):
        for k in keys:
            self.wait(eng, self.raw.get(k))

    def rdone(self, tok, *keys):
        if tok is None:
            return
        for k in keys:
            self.war.setdefault(k, []).append(tok)

    def wr(self, eng, *keys):
        for k in keys:
            for t in self.war.get(k, []):
                self.wait(eng, t)
            self.wait(eng, self.raw.get(k))

    def wdone(self, tok, *keys):
        for k in keys:
            self.raw[k] = tok
            self.war[k] = []


def _stripes(L):
    out = []
    t0 = 0
    while t0 < L:
        out.append((t0, min(SST, L - t0)))
        t0 += SST
    return out


def build(L):
    WIN = L // 2
    nblocks = L // TB
    nrounds = nblocks // 2
    stripes = _stripes(L)
    ns = len(stripes)
    NWG = WIN // 128

    nc = bass.Bass("TRN2", target_bir_lowering=False)

    x_d = nc.dram_tensor("x", [L, D], F32, kind="ExternalInput").ap()
    xres_d = nc.dram_tensor("xres", [WIN, D], F32, kind="ExternalInput").ap()
    idx_d = nc.dram_tensor("idx", [128, NWG], I32, kind="ExternalInput").ap()
    Win_d = nc.dram_tensor("W_in", [D, DI], F32, kind="ExternalInput").ap()
    cwx_d = nc.dram_tensor("cwx", [128, 32], F32, kind="ExternalInput").ap()
    cbx_d = nc.dram_tensor("cbx", [128, 8], F32, kind="ExternalInput").ap()
    cwz_d = nc.dram_tensor("cwz", [128, 32], F32, kind="ExternalInput").ap()
    cbz_d = nc.dram_tensor("cbz", [128, 8], F32, kind="ExternalInput").ap()
    Wxd_d = nc.dram_tensor("W_xdbl", [D, 96], F32, kind="ExternalInput").ap()
    Wdt_d = nc.dram_tensor("W_dt", [DTR, D], F32, kind="ExternalInput").ap()
    inv_d = nc.dram_tensor("invdt", [128, 8], F32, kind="ExternalInput").ap()
    Dv_d = nc.dram_tensor("Dv", [128, 8], F32, kind="ExternalInput").ap()
    Wout_d = nc.dram_tensor("W_out", [DI, D], F32, kind="ExternalInput").ap()
    bout_d = nc.dram_tensor("bout", [128, 8], F32, kind="ExternalInput").ap()
    rw1_d = nc.dram_tensor("rw1", [D, D], F32, kind="ExternalInput").ap()
    rb1_d = nc.dram_tensor("rb1", [128, 8], F32, kind="ExternalInput").ap()
    rw2_d = nc.dram_tensor("rw2", [D, NE], F32, kind="ExternalInput").ap()
    rb2_d = nc.dram_tensor("rb2", [NE, 1], F32, kind="ExternalInput").ap()
    we1_d = nc.dram_tensor("we1", [NE, D, HE], F32, kind="ExternalInput").ap()
    be1_d = nc.dram_tensor("be1c", [128, NE * 32], F32, kind="ExternalInput").ap()
    we2_d = nc.dram_tensor("we2", [NE, HE, D], F32, kind="ExternalInput").ap()
    be2_d = nc.dram_tensor("be2", [NE, D], F32, kind="ExternalInput").ap()
    id_d = nc.dram_tensor("ident", [128, 128], F32, kind="ExternalInput").ap()
    out_d = nc.dram_tensor("out", [WIN, D], F32, kind="ExternalOutput").ap()

    att_s = nc.dram_tensor("att_s", [L, D], BF16, kind="Internal").ap()
    xdb_s = nc.dram_tensor("xdb_s", [32, L], F32, kind="Internal").ap()
    win_s = nc.dram_tensor("win_s", [D, DI], BF16, kind="Internal").ap()
    wout_s = nc.dram_tensor("wout_s", [DI, D], BF16, kind="Internal").ap()
    rw1_s = nc.dram_tensor("rw1_s", [D, D], BF16, kind="Internal").ap()
    we1_s = nc.dram_tensor("we1_s", [NE, D, HE], BF16, kind="Internal").ap()
    we2_s = nc.dram_tensor("we2_s", [NE, HE, D], BF16, kind="Internal").ap()

    es = contextlib.ExitStack()
    sb = lambda nm, sh, dt: es.enter_context(nc.sbuf_tensor(nm, sh, dt))

    c_idx = sb("c_idx", [128, NWG], I32)
    c_cwx = sb("c_cwx", [128, 32], F32)
    c_cbx = sb("c_cbx", [128, 8], F32)
    c_cwz = sb("c_cwz", [128, 32], F32)
    c_cbz = sb("c_cbz", [128, 8], F32)
    c_inv = sb("c_inv", [128, 8], F32)
    c_inv2 = sb("c_inv2", [128, 8], F32)
    c_D = sb("c_D", [128, 8], F32)
    c_bout = sb("c_bout", [128, 8], F32)
    c_rb1 = sb("c_rb1", [128, 8], F32)
    c_rb2 = sb("c_rb2", [NE, 1], F32)
    c_be1 = sb("c_be1", [128, NE * 32], F32)
    c_id32 = sb("c_id32", [128, 128], F32)
    c_idbf = sb("c_idbf", [128, 128], BF16)
    w_xdbl = sb("w_xdbl", [128, 8 * 96], BF16)
    w_dt = sb("w_dt", [DTR, D], BF16)
    w_rw2 = sb("w_rw2", [128, 8 * NE], BF16)
    w_be2 = sb("w_be2", [NE, D], BF16)

    xst = [sb("xst0", [128, 1024], BF16)]
    attm = [sb(f"attm{i}", [128, 1024], BF16) for i in range(2)]
    xfm = sb("xfm", [128, 8 * SW], BF16)
    xsb = sb("xsb", [128, 8 * SW], BF16)
    zb = sb("zb", [128, 8 * SW], BF16)
    ub = sb("ub", [128, 8 * SST], BF16)
    zcb = sb("zcb", [128, 8 * SST], BF16)
    cacc = [sb("cacc0", [128, SW], BF16), None]
    rvt = sb("rvt", [128, SST], F32)
    dtmp = [sb("dtmp0", [128, SST], F32), None]
    a_r = [sb(f"a_r{i}", [128, 2 * SST], F32) for i in range(2)]
    b_r = [sb(f"b_r{i}", [128, 2 * SST], BF16) for i in range(2)]
    h_r = [sb(f"h_r{i}", [128, 2 * SST], BF16) for i in range(3)]
    t_r = [sb(f"t_r{i}", [128, 2 * SST], BF16) for i in range(2)]
    bc_r = [sb(f"bc_r{i}", [128, 1024], BF16) for i in range(2)]
    yfm = sb("yfm", [128, 8 * SST], F32)
    carry = [sb(f"carry{i}", [128, 128], F32) for i in range(2)]
    xdbl = sb("xdbl", [128, SW], F32)
    dtlow = sb("dtlow", [DTR, SW], BF16)
    winb = [sb(f"winb{i}", [128, DI], BF16) for i in range(2)]
    woutb = [sb(f"woutb{i}", [128, D], BF16) for i in range(2)]

    attg = [sb("attg0", [128, D], BF16)]
    xresb = [sb("xresb0", [128, D], F32)]
    yacc = sb("yacc", [128, 4 * D], BF16)
    xtf = sb("xtf", [128, 8 * TB], BF16)
    hbuf = sb("hbuf", [128, 32 * TB], BF16)
    w1b = [sb(f"w1b{i}", [128, 8 * TB], BF16) for i in range(2)]
    w2b = [sb(f"w2b{i}", [128, D], BF16) for i in range(3)]
    lgf = sb("lgf", [NE, TB], F32)
    lgt = sb("lgt", [128, 32], F32)
    wtm = sb("wtm", [128, 32], F32)
    wfmb = sb("wfmb", [NE, TB], BF16)
    rs_t = sb("rs_t", [128, 16], F32)
    eqt = sb("eqt", [128, NE], F32)
    ett = sb("ett", [128, NE], F32)
    ewt = sb("ewt", [128, NE], F32)
    outst = [sb(f"outst{i}", [128, D], F32) for i in range(2)]

    pbk = [es.enter_context(nc.psum_tensor(f"pb{i}", [128, 512], F32))
           for i in range(8)]
    pbf = [p.bitcast(BF16) for p in pbk]

    OKEY = ["outst0", "outst1"]

    P = Pb()
    TE, VE, SE, PL, SY, SC = "tensor", "vector", "scalar", "gpsimd", "sync", "scalar"

    def dma(eng, dst, src, sem, key=None, waits=()):
        for t in waits:
            P.wait(eng, t)
        if key:
            P.wr(eng, key)
        tok = P.run(eng, lambda E, d=dst, s=src: E.dma_start(out=d, in_=s),
                    inc=sem, n=16)
        if key:
            P.wdone(tok, key)
        return tok

    consts = [
        (c_idx, idx_d), (c_cwx, cwx_d), (c_cbx, cbx_d), (c_cwz, cwz_d),
        (c_cbz, cbz_d), (c_inv, inv_d), (c_D, Dv_d), (c_bout, bout_d),
        (c_rb1, rb1_d), (c_rb2, rb2_d), (c_be1, be1_d), (c_id32, id_d),
    ]
    for t, s in consts:
        dma(SY, t[:, :], s[:, :], "qconst")
    tok_const = ("qconst", P.cnt["qconst"])

    for k in range(8):
        dma(PL, w_xdbl[:, k * 96:(k + 1) * 96], Wxd_d[k * 128:(k + 1) * 128, :],
            "qsmall")
    dma(PL, w_dt[:, :], Wdt_d[:, :], "qsmall")
    for k in range(8):
        dma(PL, w_rw2[:, k * NE:(k + 1) * NE], rw2_d[k * 128:(k + 1) * 128, :],
            "qsmall")
    dma(PL, w_be2[:, :], be2_d[:, :], "qsmall")
    tok_small = ("qsmall", P.cnt["qsmall"])

    dma(PL, win_s[:, :], Win_d[:, :], "qstgS")
    dma(PL, wout_s[:, :], Wout_d[:, :], "qstgS")
    dma(PL, rw1_s[:, :], rw1_d[:, :], "qstgS")
    tok_stgS = ("qstgS", P.cnt["qstgS"])
    stage_chunks = [(we1_s[e], we1_d[e], we2_s[e], we2_d[e]) for e in range(NE)]

    P.wait(VE, tok_const)
    t = P.run(VE, lambda E: E.tensor_add(c_inv2[:, :], c_inv[:, :], c_inv[:, :]),
              inc="cve")
    P.wdone(t, "c_inv2")
    P.wait(SE, tok_const)
    t = P.run(SE, lambda E: E.copy(c_idbf[:, :], c_id32[:, :]), inc="cse")
    P.wdone(t, "c_idbf")

    def emit_stripe(s, t0, slen, scr_out):
        W = slen + 3
        ngr = (W + 127) // 128

        for g in range(ngr):
            gs = 0
            tk0 = t0 - 1 + g * 128
            r0, r1 = max(tk0, 0), min(tk0 + 128, L)
            p0 = r0 - tk0
            part = (p0 != 0) or (r1 - r0 < 128)
            if part:
                P.wr(VE, f"xst{gs}")
                tm0 = P.run(VE, lambda E, gs=gs: E.memset(xst[gs][:, :], 0.0),
                            inc="cve")
                P.wdone(tm0, f"xst{gs}")
            P.wr(PL, f"xst{gs}")
            tok = P.run(PL, lambda E, r0=r0, r1=r1, p0=p0, gs=gs: E.dma_start(
                out=xst[gs][p0:p0 + (r1 - r0), :], in_=x_d[r0:r1, :]),
                inc=f"qx{gs}", n=16)
            P.wdone(tok, f"xst{gs}")
            P.wait(TE, tok)
            P.rd(TE, f"xst{gs}", "c_idbf")
            P.wr(TE, "pb7")
            for dt in range(8):
                tt = P.run(TE, lambda E, dt=dt, gs=gs: E.matmul(
                    pbf[7][:, dt * 128:(dt + 1) * 128],
                    xst[gs][:, dt * 128:(dt + 1) * 128],
                    c_idbf[:, :], start=(dt == 0), stop=(dt == 7),
                    is_transpose=True), inc="cte" if dt == 7 else None)
            P.wdone(tt, "pb7")
            P.rdone(tt, f"xst{gs}")
            gw = min(W - g * 128, 128)
            for dt in range(8):
                P.rd(SE, "pb7")
                P.wr(SE, f"xfm{dt}")
                te = P.run(SE, lambda E, dt=dt, g=g, gw=gw: E.copy(
                    xfm[:, dt * SW + g * 128:dt * SW + g * 128 + gw],
                    pbf[7][:, dt * 128:dt * 128 + gw]), inc="cse")
                P.rdone(te, "pb7")
                P.wdone(te, f"xfm{dt}")
            yield

        P.wait(SY, tok_stgS)
        for mh in range(2):
            for k in range(8):
                kb = k % 2
                dma(SY, winb[kb][:, :], win_s[k * 128:(k + 1) * 128, :],
                    f"qwin{kb}", key=f"winb{kb}")
                P.rd(TE, f"winb{kb}")
                if k == 0:
                    P.rd(TE, *[f"xfm{d}" for d in range(8)])
                for m8 in range(8):
                    m = mh * 8 + m8
                    if k == 0:
                        P.wr(TE, f"pb{m8}")
                    tt = P.run(TE, lambda E, k=k, m=m, kb=kb, m8=m8, W=W:
                               E.matmul(
                        pbk[m8][:, 0:W],
                        winb[kb][:, m * 128:(m + 1) * 128],
                        xfm[:, k * SW:k * SW + W],
                        start=(k == 0), stop=(k == 7)),
                        inc="cte" if (m8 == 7) else None)
                    if k == 7:
                        P.wdone((("cte", P.cnt["cte"] + 1)
                                 if m8 < 7 else tt), f"pb{m8}")
                P.rdone(tt, f"winb{kb}")
                for d in range(8):
                    P.rdone(tt, f"xfm{d}")
            for m8 in range(8):
                m = mh * 8 + m8
                dst = xsb if m < 8 else zb
                dkey = f"{'xsb' if m < 8 else 'zb'}{m % 8}"
                dcol = (m % 8) * SW
                P.rd(SE, f"pb{m8}")
                P.wr(SE, dkey)
                te = P.run(SE, lambda E, dst=dst, dcol=dcol, m8=m8, W=W: E.copy(
                    dst[:, dcol:dcol + W], pbk[m8][:, 0:W]), inc="cse")
                P.rdone(te, f"pb{m8}")
                P.wdone(te, dkey)
            yield

        for path, src, wsc, bsc, dst, dkey in (
                ("x", xsb, c_cwx, c_cbx, ub, "ub"),
                ("z", zb, c_cwz, c_cbz, zcb, "zcb")):
            skey = "xsb" if path == "x" else "zb"
            for dt in range(8):
                ca = cacc[0]
                ckey = "cacc0"
                P.rd(VE, f"{skey}{dt}")
                P.wait(VE, tok_const)
                P.wr(VE, ckey)
                tv = P.run(VE, lambda E, src=src, dt=dt, wsc=wsc, slen=slen,
                           ca=ca: E.tensor_scalar_mul(
                    ca[:, 0:slen], src[:, dt * SW:dt * SW + slen],
                    wsc[:, dt * 4:dt * 4 + 1]), inc="cve")
                P.wdone(tv, ckey)
                for j in range(1, 4):
                    P.wr(VE, ckey)
                    tv = P.run(VE, lambda E, src=src, dt=dt, j=j, wsc=wsc,
                               slen=slen, ca=ca: E.scalar_tensor_tensor(
                        ca[:, 0:slen],
                        src[:, dt * SW + j:dt * SW + j + slen],
                        wsc[:, dt * 4 + j:dt * 4 + j + 1],
                        ca[:, 0:slen], ALU.mult, ALU.add), inc="cve")
                    P.wdone(tv, ckey)
                P.rdone(tv, f"{skey}{dt}")
                P.rd(SE, ckey)
                P.wr(SE, f"{dkey}{dt}")
                te = P.run(SE, lambda E, dst=dst, dt=dt, bsc=bsc, slen=slen,
                           ca=ca: E.activation(
                    dst[:, dt * SST:dt * SST + slen], ca[:, 0:slen], AF.Silu,
                    bias=bsc[:, dt:dt + 1]), inc="cse")
                P.rdone(te, ckey)
                P.wdone(te, f"{dkey}{dt}")
                if dt % 4 == 3:
                    yield

        P.wait(TE, tok_small)
        for k in range(8):
            P.rd(TE, f"ub{k}")
            if k == 0:
                P.wr(TE, "pb0")
            tt = P.run(TE, lambda E, k=k, slen=slen: E.matmul(
                pbk[0][0:96, 0:slen], w_xdbl[:, k * 96:(k + 1) * 96],
                ub[:, k * SST:k * SST + slen],
                start=(k == 0), stop=(k == 7)), inc="cte")
            P.rdone(tt, f"ub{k}")
        P.wdone(tt, "pb0")
        P.rd(VE, "pb0")
        P.wr(VE, "xdbl")
        tv = P.run(VE, lambda E, slen=slen: E.tensor_copy(
            xdbl[0:96, 0:slen], pbk[0][0:96, 0:slen]), inc="cve")
        P.rdone(tv, "pb0")
        P.wdone(tv, "xdbl")
        P.rd(SE, "xdbl")
        P.wr(SE, "dtlow")
        te = P.run(SE, lambda E, slen=slen: E.copy(
            dtlow[:, 0:slen], xdbl[0:DTR, 0:slen]), inc="cse")
        P.wdone(te, "dtlow")
        P.rd(SC, "xdbl")
        tk = P.run(SC, lambda E, t0=t0, slen=slen: E.dma_start(
            out=xdb_s[:, t0:t0 + slen], in_=xdbl[64:96, 0:slen]),
            inc="qxdbl", n=16)
        P.rdone(tk, "xdbl")
        P.wait(SC, tk)
        tok_xd = P.mark(SC, "cxd")
        yield

        for m in range(8):
            bank = 1 + (m % 2)
            P.rd(TE, "dtlow")
            P.wr(TE, f"pb{bank}")
            tt = P.run(TE, lambda E, m=m, bank=bank, slen=slen: E.matmul(
                pbk[bank][:, 0:slen], w_dt[:, m * 128:(m + 1) * 128],
                dtlow[:, 0:slen], start=True, stop=True), inc="cte")
            P.rdone(tt, "dtlow")
            P.wdone(tt, f"pb{bank}")
            P.rd(SE, f"pb{bank}", "c_inv2")
            P.wr(SE, "dtmp0")
            ta = P.run(SE, lambda E, m=m, bank=bank, slen=slen: E.activation(
                dtmp[0][:, 0:slen], pbk[bank][:, 0:slen], AF.Abs,
                bias=c_inv2[:, m:m + 1]), inc="cse")
            P.wdone(ta, "dtmp0")
            P.wr(SE, "rvt")
            tr = P.run(SE, lambda E, m=m, bank=bank, slen=slen: E.activation(
                rvt[:, 0:slen], pbk[bank][:, 0:slen], AF.Relu,
                bias=c_inv2[:, m:m + 1]), inc="cse")
            P.rdone(tr, f"pb{bank}")
            P.wdone(tr, "rvt")
            P.rd(SE, "dtmp0")
            tx = P.run(SE, lambda E, slen=slen: E.activation(
                dtmp[0][:, 0:slen], dtmp[0][:, 0:slen], AF.Exp, scale=-1.0),
                inc="cse")
            P.wdone(tx, "dtmp0")
            P.rd(SE, "dtmp0")
            tl = P.run(SE, lambda E, slen=slen: E.activation(
                dtmp[0][:, 0:slen], dtmp[0][:, 0:slen], AF.Ln, bias=1.0),
                inc="cse")
            P.wdone(tl, "dtmp0")
            P.rd(VE, "dtmp0", "rvt")
            P.wr(VE, f"xfm{m}")
            tv = P.run(VE, lambda E, m=m, slen=slen: E.tensor_add(
                xfm[:, m * SW:m * SW + slen], rvt[:, 0:slen],
                dtmp[0][:, 0:slen]), inc="cve")
            P.rdone(tv, "dtmp0")
            P.rdone(tv, "rvt")
            P.wdone(tv, f"xfm{m}")
            if m % 4 == 3:
                yield

        for dt in range(8):
            P.rd(VE, f"ub{dt}")
            P.wr(VE, f"yfm{dt}")
            tv = P.run(VE, lambda E, dt=dt, slen=slen: E.tensor_scalar_mul(
                yfm[:, dt * SST:dt * SST + slen],
                ub[:, dt * SST:dt * SST + slen], c_D[:, dt:dt + 1]),
                inc="cve")
            P.rdone(tv, f"ub{dt}")
            P.wdone(tv, f"yfm{dt}")
            P.rd(VE, f"xfm{dt}", f"ub{dt}")
            P.wr(VE, f"ub{dt}")
            tv = P.run(VE, lambda E, dt=dt, slen=slen: E.tensor_mul(
                ub[:, dt * SST:dt * SST + slen],
                xfm[:, dt * SW:dt * SW + slen],
                ub[:, dt * SST:dt * SST + slen]), inc="cve")
            P.rdone(tv, f"xfm{dt}")
            P.wdone(tv, f"ub{dt}")

        for n in range(NST):
            bslot = n % 2
            bc = bc_r[bslot]
            P.wait(PL, tok_xd)
            P.wr(PL, f"bc{bslot}")
            tb1 = P.run(PL, lambda E, n=n, t0=t0, slen=slen, bc=bc: E.dma_start(
                out=bc[:, 0:slen],
                in_=xdb_s[n:n + 1, t0:t0 + slen].broadcast_to([128, slen])),
                inc=f"qbcB{bslot}", n=16)
            tb2 = P.run(PL, lambda E, n=n, t0=t0, slen=slen, bc=bc: E.dma_start(
                out=bc[:, 512:512 + slen],
                in_=xdb_s[16 + n:17 + n, t0:t0 + slen].broadcast_to([128, slen])),
                inc=f"qbcC{bslot}", n=16)
            P.wdone(tb2, f"bc{bslot}")
            P.raw[f"bcB{bslot}"] = tb1
            for j in range(4):
                i2 = n * 4 + j
                asl, hsl, tsl = i2 % 2, i2 % 3, i2 % 2
                d0, d1 = 2 * j, 2 * j + 1
                dpk = (f"ub{d0}", f"ub{d1}")
                dv = xfm[:, d0 * SW:(d1 + 1) * SW].rearrange(
                    "p (a m) -> p a m", a=2)[:, :, 0:slen]
                P.rd(SE, f"xfm{d0}", f"xfm{d1}")
                P.wr(SE, f"a{asl}")
                av = a_r[asl][:, 0:2 * SST].rearrange(
                    "p (a m) -> p a m", a=2)[:, :, 0:slen]
                ta = P.run(SE, lambda E, n=n, dv=dv, av=av: E.activation(
                    av, dv, AF.Exp, scale=-(n + 1.0)), inc="cse")
                P.wdone(ta, f"a{asl}")
                duv = ub[:, d0 * SST:d0 * SST + 2 * SST].rearrange(
                    "p (a m) -> p a m", a=2)[:, :, 0:slen]
                bbv = bc[:, 0:slen].rearrange("p (a m) -> p a m", a=1
                                              ).broadcast_to([128, 2, slen])
                bv = b_r[asl][:, 0:2 * SST].rearrange(
                    "p (a m) -> p a m", a=2)[:, :, 0:slen]
                P.rd(VE, *dpk, f"bc{bslot}", f"bcB{bslot}")
                P.wr(VE, f"b{asl}")
                tbv = P.run(VE, lambda E, duv=duv, bbv=bbv, bv=bv:
                            E.tensor_tensor(bv, duv, bbv, ALU.mult), inc="cve")
                P.rdone(tbv, *dpk)
                P.wdone(tbv, f"b{asl}")
                for q in range(2):
                    i = n * 8 + d0 + q
                    P.rd(VE, f"a{asl}", f"b{asl}")
                    if s > 0:
                        P.rd(VE, f"car{s % 2}_{i}")
                    P.wr(VE, f"h{hsl}")
                    init = 0.0 if s == 0 else carry[s % 2][:, i:i + 1]
                    th = P.run(VE, lambda E, asl=asl, hsl=hsl, q=q, init=init,
                               slen=slen: E.tensor_tensor_scan(
                        h_r[hsl][:, q * SST:q * SST + slen],
                        a_r[asl][:, q * SST:q * SST + slen],
                        b_r[asl][:, q * SST:q * SST + slen],
                        init, ALU.mult, ALU.add), inc="cve")
                    P.rdone(th, f"a{asl}")
                    P.rdone(th, f"b{asl}")
                    if s > 0:
                        P.rdone(th, f"car{s % 2}_{i}")
                    P.wdone(th, f"h{hsl}")
                if s + 1 < ns:
                    i = n * 8 + d0
                    hv = h_r[hsl][:, 0:2 * SST].rearrange(
                        "p (a m) -> p a m", a=2)[:, :, slen - 1:slen]
                    P.rd(VE, f"h{hsl}")
                    P.wr(VE, f"car{(s + 1) % 2}_{i}", f"car{(s + 1) % 2}_{i + 1}")
                    tc = P.run(VE, lambda E, hv=hv, i=i, s=s: E.tensor_copy(
                        carry[(s + 1) % 2][:, i:i + 2], hv), inc="cve")
                    P.wdone(tc, f"car{(s + 1) % 2}_{i}",
                            f"car{(s + 1) % 2}_{i + 1}")
                cbv = bc[:, 512:512 + slen].rearrange(
                    "p (a m) -> p a m", a=1).broadcast_to([128, 2, slen])
                hvv = h_r[hsl][:, 0:2 * SST].rearrange(
                    "p (a m) -> p a m", a=2)[:, :, 0:slen]
                tvv = t_r[tsl][:, 0:2 * SST].rearrange(
                    "p (a m) -> p a m", a=2)[:, :, 0:slen]
                P.rd(VE, f"h{hsl}", f"bc{bslot}")
                P.wr(VE, f"t{tsl}")
                tp = P.run(VE, lambda E, hvv=hvv, cbv=cbv, tvv=tvv:
                           E.tensor_tensor(tvv, hvv, cbv, ALU.mult), inc="cve")
                P.rdone(tp, f"h{hsl}")
                P.rdone(tp, f"bc{bslot}")
                P.wdone(tp, f"t{tsl}")
                yk = (f"yfm{d0}", f"yfm{d1}")
                P.rd(PL, f"t{tsl}", *yk)
                P.wr(PL, *yk)
                ty = P.run(PL, lambda E, d0=d0, tsl=tsl: E.tensor_tensor(
                    yfm[:, d0 * SST:(d0 + 2) * SST],
                    yfm[:, d0 * SST:(d0 + 2) * SST],
                    t_r[tsl][:, 0:2 * SST], ALU.add), inc="cpl")
                P.rdone(ty, f"t{tsl}")
                P.wdone(ty, *yk)
                yield

        for dt in range(8):
            P.rd(SE, f"yfm{dt}")
            P.wr(SE, f"xsb{dt}")
            te = P.run(SE, lambda E, dt=dt, slen=slen: E.copy(
                xsb[:, dt * SW:dt * SW + slen],
                yfm[:, dt * SST:dt * SST + slen]), inc="cse")
            P.rdone(te, f"yfm{dt}")
            P.wdone(te, f"xsb{dt}")
        yield

        for _ in range(24):
            yield
        for ki, k in enumerate(list(range(8, 16)) + list(range(8))):
            kb = ki % 2
            dma(SY, woutb[kb][:, :], wout_s[k * 128:(k + 1) * 128, :],
                f"qwout{kb}", key=f"woutb{kb}")
            P.rd(TE, f"woutb{kb}")
            if k < 8:
                P.rd(TE, f"xsb{k}")
            else:
                P.rd(TE, f"zcb{k - 8}")
            for m in range(8):
                if ki == 0:
                    P.wr(TE, f"pb{m}")
                rhs = (xsb[:, k * SW:k * SW + slen] if k < 8
                       else zcb[:, (k - 8) * SST:(k - 8) * SST + slen])
                tt = P.run(TE, lambda E, k=k, m=m, kb=kb, rhs=rhs, ki=ki:
                           E.matmul(
                    pbk[m][:, 0:slen], woutb[kb][:, m * 128:(m + 1) * 128],
                    rhs, start=(ki == 0), stop=(ki == 15)),
                    inc="cte" if m == 7 else None)
                if ki == 15:
                    P.wdone((("cte", P.cnt["cte"] + 1)
                             if m < 7 else tt), f"pb{m}")
            P.rdone(tt, f"woutb{kb}")
            if k < 8:
                P.rdone(tt, f"xsb{k}")
            else:
                P.rdone(tt, f"zcb{k - 8}")
        for m in range(8):
            P.rd(SE, f"pb{m}")
            P.wr(SE, f"zb{m}")
            te = P.run(SE, lambda E, m=m, slen=slen: E.activation(
                zb[:, m * SW:m * SW + slen], pbk[m][:, 0:slen], AF.Identity,
                bias=c_bout[:, m:m + 1]), inc="cse")
            P.rdone(te, f"pb{m}")
            P.wdone(te, f"zb{m}")

        ngo = (slen + 127) // 128
        for g in range(ngo):
            gw = min(128, slen - g * 128)
            aslot = g % 2
            akey = f"attm{aslot}"
            P.wr(TE, "pb7")
            for dt in range(8):
                P.rd(TE, f"zb{dt}")
                tt = P.run(TE, lambda E, dt=dt, g=g, gw=gw: E.matmul(
                    pbf[7][0:gw, dt * 128:dt * 128 + 128],
                    zb[:, dt * SW + g * 128:dt * SW + g * 128 + gw],
                    c_idbf[:, :], start=(dt == 0), stop=(dt == 7),
                    is_transpose=True), inc="cte")
                P.rdone(tt, f"zb{dt}")
            P.wdone(tt, "pb7")
            P.rd(VE, "pb7")
            P.wr(VE, akey)
            tv = P.run(VE, lambda E, aslot=aslot, gw=gw: E.tensor_copy(
                attm[aslot][0:gw, :], pbf[7][0:gw, :]), inc="cve")
            P.rdone(tv, "pb7")
            P.wdone(tv, akey)
            P.rd(SC, akey)
            tk = P.run(SC, lambda E, t0=t0, g=g, gw=gw, aslot=aslot:
                       E.dma_start(
                out=att_s[t0 + g * 128:t0 + g * 128 + gw, :],
                in_=attm[aslot][0:gw, :]), inc=f"qscr{aslot}", n=16)
            P.rdone(tk, akey)
            yield
        for sl in range(2):
            if f"qscr{sl}" in P.cnt:
                P.wait(SC, (f"qscr{sl}", P.cnt[f"qscr{sl}"]))
        scr_out.append(P.mark(SC, "cscr"))

    def emit_round(r, scr_list):
        for g in range(4):
            gs = g % 2
            need = (2 * r + 1) * TB + (g + 1) * 128
            sidx = min((need + SST - 1) // SST - 1, ns - 1)
            P.wait(PL, scr_list[sidx])
            P.wait(PL, tok_const)
            P.wr(PL, "attg0")
            tg = P.run(PL, lambda E, r=r, g=g: E.indirect_dma_start(
                attg[0][:, :], None, att_s[:, :],
                IndirectOffsetOnAxis(c_idx[:, r * 4 + g:r * 4 + g + 1], 0)),
                inc="qgat0", n=16)
            P.wdone(tg, "attg0")
            P.wr(SC, "xres0")
            tr = P.run(SC, lambda E, r=r, g=g: E.dma_start(
                out=xresb[0][:, :],
                in_=xres_d[r * TB + g * 128:r * TB + g * 128 + 128, :]),
                inc=f"qres{gs}", n=16)
            P.wdone(tr, "xres0")
            P.rd(VE, "attg0", "xres0")
            P.wr(VE, "xres0")
            tv = P.run(VE, lambda E: E.tensor_add(
                xresb[0][:, :], xresb[0][:, :], attg[0][:, :]), inc="cve")
            P.rdone(tv, "attg0")
            P.wdone(tv, "xres0")
            P.rd(VE, "xres0")
            t1 = P.run(VE, lambda E: E.reduce_sum(
                rs_t[:, 0:1], xresb[0][:, :], axis=AX), inc="cve")
            P.rd(SE, "xres0")
            P.wr(SE, OKEY[gs])
            t2 = P.run(SE, lambda E, gs=gs: E.activation(
                outst[gs][:, :], xresb[0][:, :], AF.Square), inc="cse")
            P.wdone(t2, OKEY[gs])
            P.rd(VE, OKEY[gs])
            t3 = P.run(VE, lambda E, gs=gs: E.reduce_sum(
                rs_t[:, 1:2], outst[gs][:, :], axis=AX), inc="cve")
            P.rdone(t3, OKEY[gs])
            P.wait(VE, t1)
            t4 = P.run(VE, lambda E: E.tensor_single_scalar(
                rs_t[:, 2:3], rs_t[:, 0:1], 1.0 / D, ALU.mult), inc="cve")
            P.wait(VE, t3)
            t5 = P.run(VE, lambda E: E.tensor_single_scalar(
                rs_t[:, 3:4], rs_t[:, 1:2], 1.0 / D, ALU.mult), inc="cve")
            P.wait(VE, t4)
            t6 = P.run(VE, lambda E: E.tensor_mul(
                rs_t[:, 4:5], rs_t[:, 2:3], rs_t[:, 2:3]), inc="cve")
            P.wait(VE, t5)
            P.wait(VE, t6)
            t7 = P.run(VE, lambda E: E.tensor_sub(
                rs_t[:, 5:6], rs_t[:, 3:4], rs_t[:, 4:5]), inc="cve")
            P.wait(VE, t7)
            t7b = P.run(VE, lambda E: E.tensor_single_scalar(
                rs_t[:, 5:6], rs_t[:, 5:6], 1e-5, ALU.add), inc="cve")
            P.wait(SE, t7b)
            t8 = P.run(SE, lambda E: E.activation(
                rs_t[:, 6:7], rs_t[:, 5:6], AF.Sqrt), inc="cse")
            P.wait(VE, t8)
            t9 = P.run(VE, lambda E: E.reciprocal(rs_t[:, 7:8], rs_t[:, 6:7]),
                       inc="cve")
            P.wait(VE, t9)
            ta = P.run(VE, lambda E: E.tensor_single_scalar(
                rs_t[:, 8:9], rs_t[:, 7:8], 2.0, ALU.mult), inc="cve")
            P.wait(VE, ta)
            P.rd(VE, "xres0")
            P.wr(VE, f"yacc{g}")
            tb3 = P.run(VE, lambda E, g=g: E.tensor_scalar(
                yacc[:, g * D:(g + 1) * D], xresb[0][:, :], rs_t[:, 2:3],
                rs_t[:, 8:9], ALU.subtract, ALU.mult), inc="cve")
            P.rdone(tb3, "xres0")
            P.wdone(tb3, f"yacc{g}")
            yield

        for dt in range(8):
            P.wr(TE, "pb7")
            for g in range(4):
                P.rd(TE, f"yacc{g}")
                tt = P.run(TE, lambda E, dt=dt, g=g: E.matmul(
                    pbf[7][:, g * 128:(g + 1) * 128],
                    yacc[:, g * D + dt * 128:g * D + dt * 128 + 128],
                    c_idbf[:, :], start=(g == 0), stop=(g == 3),
                    is_transpose=True), inc="cte")
                P.rdone(tt, f"yacc{g}")
            P.wdone(tt, "pb7")
            P.rd(SE, "pb7")
            P.wr(SE, f"xtf{dt}")
            te = P.run(SE, lambda E, dt=dt: E.copy(
                xtf[:, dt * TB:(dt + 1) * TB], pbf[7][:, 0:TB]), inc="cse")
            P.rdone(te, "pb7")
            P.wdone(te, f"xtf{dt}")
            if dt % 4 == 3:
                yield

        for k in range(8):
            kb = k % 3
            dma(SC, w2b[kb][:, :], rw1_s[k * 128:(k + 1) * 128, :],
                f"qw2{kb}", key=f"w2b{kb}", waits=(tok_stgS,))
            P.rd(TE, f"w2b{kb}", f"xtf{k}")
            for m in range(8):
                if k == 0:
                    P.wr(TE, f"pb{m}")
                tt = P.run(TE, lambda E, k=k, m=m, kb=kb: E.matmul(
                    pbk[m][:, 0:TB], w2b[kb][:, m * 128:(m + 1) * 128],
                    xtf[:, k * TB:(k + 1) * TB],
                    start=(k == 0), stop=(k == 7)),
                    inc="cte" if m == 7 else None)
                if k == 7:
                    P.wdone((("cte", P.cnt["cte"] + 1)
                             if m < 7 else tt), f"pb{m}")
            P.rdone(tt, f"w2b{kb}")
            P.rdone(tt, f"xtf{k}")
        for m in range(8):
            P.rd(SE, f"pb{m}")
            P.wr(SE, f"h{m}")
            te = P.run(SE, lambda E, m=m: E.activation(
                hbuf[:, m * TB:(m + 1) * TB], pbk[m][:, 0:TB], AF.Silu,
                bias=c_rb1[:, m:m + 1], scale=0.5), inc="cse")
            P.rdone(te, f"pb{m}")
            P.wdone(te, f"h{m}")
        P.wait(TE, tok_small)
        for k in range(8):
            P.rd(TE, f"h{k}")
            if k == 0:
                P.wr(TE, "pb0")
            tt = P.run(TE, lambda E, k=k: E.matmul(
                pbk[0][0:NE, 0:TB], w_rw2[:, k * NE:(k + 1) * NE],
                hbuf[:, k * TB:(k + 1) * TB], start=(k == 0), stop=(k == 7)),
                inc="cte")
            P.rdone(tt, f"h{k}")
        P.wdone(tt, "pb0")
        P.rd(VE, "pb0")
        P.wr(VE, "lgf")
        tv = P.run(VE, lambda E: E.tensor_scalar_add(
            lgf[:, :], pbk[0][0:NE, 0:TB], c_rb2[:, 0:1]), inc="cve")
        P.rdone(tv, "pb0")
        P.wdone(tv, "lgf")
        P.wr(TE, "pb1")
        P.rd(TE, "lgf")
        for g in range(4):
            tt = P.run(TE, lambda E, g=g: E.matmul(
                pbk[1][:, g * 8:g * 8 + 8], lgf[:, g * 128:(g + 1) * 128],
                c_id32[0:8, 0:8], start=(g == 0), stop=(g == 3),
                is_transpose=True), inc="cte" if g == 3 else None)
        P.rdone(tt, "lgf")
        P.wdone(tt, "pb1")
        P.rd(VE, "pb1")
        P.wr(VE, "lgt")
        tv = P.run(VE, lambda E: E.tensor_copy(lgt[:, :], pbk[1][:, 0:32]),
                   inc="cve")
        P.rdone(tv, "pb1")
        P.wdone(tv, "lgt")
        for g in range(4):
            lg = lgt[:, g * 8:(g + 1) * 8]
            P.rd(VE, "lgt")
            t1 = P.run(VE, lambda E, lg=lg: E.reduce_max(
                rs_t[:, 9:10], lg, axis=AX), inc="cve")
            P.wait(VE, t1)
            t2 = P.run(VE, lambda E, lg=lg: E.tensor_single_scalar(
                eqt[:, :], lg, rs_t[:, 9:10], ALU.is_equal), inc="cve")
            P.wait(VE, t2)
            t3 = P.run(VE, lambda E, lg=lg: E.scalar_tensor_tensor(
                ewt[:, :], eqt[:, :], -1e30, lg, ALU.mult, ALU.add),
                inc="cve")
            P.wait(VE, t3)
            t4 = P.run(VE, lambda E: E.reduce_max(
                rs_t[:, 10:11], ewt[:, :], axis=AX), inc="cve")
            t5 = P.run(VE, lambda E: E.tensor_single_scalar(
                rs_t[:, 11:12], rs_t[:, 9:10], -1.0, ALU.mult), inc="cve")
            P.wait(SE, t5)
            t6 = P.run(SE, lambda E, lg=lg: E.activation(
                ett[:, :], lg, AF.Exp, bias=rs_t[:, 11:12]), inc="cse")
            P.wait(VE, t4)
            t7 = P.run(VE, lambda E, lg=lg: E.tensor_single_scalar(
                eqt[:, :], lg, rs_t[:, 10:11], ALU.is_ge), inc="cve")
            P.wait(VE, t6)
            P.wait(VE, t7)
            t8 = P.run(VE, lambda E: E.tensor_mul(
                ewt[:, :], ett[:, :], eqt[:, :]), inc="cve")
            P.wait(VE, t8)
            t9 = P.run(VE, lambda E: E.reduce_sum(
                rs_t[:, 12:13], ewt[:, :], axis=AX), inc="cve")
            P.wait(VE, t9)
            ta = P.run(VE, lambda E: E.reciprocal(
                rs_t[:, 13:14], rs_t[:, 12:13]), inc="cve")
            P.wait(VE, ta)
            P.wr(VE, f"wtm{g}")
            tb3 = P.run(VE, lambda E, g=g: E.tensor_scalar_mul(
                wtm[:, g * 8:(g + 1) * 8], ewt[:, :], rs_t[:, 13:14]),
                inc="cve")
            P.wdone(tb3, f"wtm{g}")
        P.wr(TE, "pb2")
        for g in range(4):
            P.rd(TE, f"wtm{g}")
            tt = P.run(TE, lambda E, g=g: E.matmul(
                pbk[2][0:8, g * 128:(g + 1) * 128], wtm[:, g * 8:(g + 1) * 8],
                c_id32[:, :], start=(g == 0), stop=(g == 3),
                is_transpose=True), inc="cte" if g == 3 else None)
            P.rdone(tt, f"wtm{g}")
        P.wdone(tt, "pb2")
        P.rd(SE, "pb2")
        P.wr(SE, "wfmb")
        te = P.run(SE, lambda E: E.copy(wfmb[:, :], pbk[2][0:8, 0:TB]),
                   inc="cse")
        P.rdone(te, "pb2")
        P.wdone(te, "wfmb")
        for g in range(4):
            for hh in range(2):
                bank = 3 + hh
                P.rd(TE, "wfmb")
                P.wr(TE, f"pb{bank}")
                tt = P.run(TE, lambda E, g=g, hh=hh, bank=bank: E.matmul(
                    pbk[bank][:, 0:TB], wfmb[:, g * 128:(g + 1) * 128],
                    w_be2[:, hh * TB:(hh + 1) * TB], start=True, stop=True),
                    inc="cte")
                P.rdone(tt, "wfmb")
                P.wdone(tt, f"pb{bank}")
                P.rd(VE, f"pb{bank}", f"yacc{g}")
                P.wr(VE, f"yacc{g}")
                tv = P.run(VE, lambda E, g=g, hh=hh, bank=bank: E.tensor_add(
                    yacc[:, g * D + hh * TB:g * D + (hh + 1) * TB],
                    yacc[:, g * D + hh * TB:g * D + (hh + 1) * TB],
                    pbk[bank][:, 0:TB]), inc="cve")
                P.rdone(tv, f"pb{bank}")
                P.wdone(tv, f"yacc{g}")
        yield

        for e in range(NE):
            for mg in range(8):
                mb = mg % 2
                src1 = we1_s[e][:, mg * TB:(mg + 1) * TB].rearrange(
                    "(k p) m -> p k m", p=128)
                for half in range(2):
                    dst = w1b[mb][:, half * 2048:(half + 1) * 2048].rearrange(
                        "p (k m) -> p k m", k=4)
                    dma(SC, dst, src1[:, half * 4:(half + 1) * 4, :],
                        f"qw1{mb}", key=f"w1b{mb}",
                        waits=(("qstgE", 16 * 2 * NE),))
                P.rd(TE, f"w1b{mb}")
                for mt in range(4):
                    bank = 5 + (mt % 2)
                    P.wr(TE, f"pb{bank}")
                    for k in range(8):
                        P.rd(TE, f"xtf{k}")
                        tt = P.run(TE, lambda E, mb=mb, mt=mt, k=k, bank=bank:
                                   E.matmul(
                            pbk[bank][:, 0:TB],
                            w1b[mb][:, k * TB + mt * 128:k * TB + mt * 128 + 128],
                            xtf[:, k * TB:(k + 1) * TB],
                            start=(k == 0), stop=(k == 7)),
                            inc="cte" if k == 7 else None)
                    P.wdone(tt, f"pb{bank}")
                    hm = mg * 4 + mt
                    P.rd(SE, f"pb{bank}")
                    P.wr(SE, f"h{hm}")
                    te = P.run(SE, lambda E, e=e, hm=hm, bank=bank:
                               E.activation(
                        hbuf[:, hm * TB:(hm + 1) * TB], pbk[bank][:, 0:TB],
                        AF.Silu, bias=c_be1[:, e * 32 + hm:e * 32 + hm + 1],
                        scale=0.5), inc="cse")
                    P.rdone(te, f"pb{bank}")
                    P.wdone(te, f"h{hm}")
                    yield
                P.rdone(tt, f"w1b{mb}")
            for tt4 in range(4):
                for hh in range(2):
                    P.wr(TE, f"pb{tt4 * 2 + hh}")
            for k in range(32):
                kb = k % 3
                dma(SC, w2b[kb][:, :], we2_s[e][k * 128:(k + 1) * 128, :],
                    f"qw2{kb}", key=f"w2b{kb}")
                P.rd(TE, f"w2b{kb}", f"h{k}")
                for tt4 in range(4):
                    for hh in range(2):
                        tt = P.run(TE, lambda E, k=k, kb=kb, tt4=tt4, hh=hh:
                                   E.matmul(
                            pbk[tt4 * 2 + hh][:, 0:TB],
                            hbuf[:, k * TB + tt4 * 128:k * TB + tt4 * 128 + 128],
                            w2b[kb][:, hh * TB:(hh + 1) * TB],
                            start=(k == 0), stop=(k == 31)),
                            inc="cte" if (tt4 == 3 and hh == 1) else None)
                P.rdone(tt, f"w2b{kb}")
                P.rdone(tt, f"h{k}")
            tlast = tt
            for tt4 in range(4):
                for hh in range(2):
                    bank = tt4 * 2 + hh
                    P.wait(VE, tlast)
                    P.rd(VE, f"pb{bank}", f"yacc{tt4}", f"wtm{tt4}")
                    P.wr(VE, f"yacc{tt4}")
                    tv = P.run(VE, lambda E, e=e, tt4=tt4, hh=hh, bank=bank:
                               E.scalar_tensor_tensor(
                        yacc[:, tt4 * D + hh * TB:tt4 * D + (hh + 1) * TB],
                        pbk[bank][:, 0:TB],
                        wtm[:, tt4 * 8 + e:tt4 * 8 + e + 1],
                        yacc[:, tt4 * D + hh * TB:tt4 * D + (hh + 1) * TB],
                        ALU.mult, ALU.add), inc="cve")
                    P.rdone(tv, f"pb{bank}")
                    P.wdone(tv, f"yacc{tt4}")
            yield

        for g in range(4):
            gs = g % 2
            P.rd(VE, f"yacc{g}")
            t1 = P.run(VE, lambda E, g=g: E.reduce_sum(
                rs_t[:, 0:1], yacc[:, g * D:(g + 1) * D], axis=AX), inc="cve")
            P.rd(SE, f"yacc{g}")
            P.wr(SE, OKEY[gs])
            t2 = P.run(SE, lambda E, g=g, gs=gs: E.activation(
                outst[gs][:, :], yacc[:, g * D:(g + 1) * D], AF.Square),
                inc="cse")
            P.wdone(t2, OKEY[gs])
            P.rd(VE, OKEY[gs])
            t3 = P.run(VE, lambda E, gs=gs: E.reduce_sum(
                rs_t[:, 1:2], outst[gs][:, :], axis=AX), inc="cve")
            P.rdone(t3, OKEY[gs])
            P.wait(VE, t1)
            t4 = P.run(VE, lambda E: E.tensor_single_scalar(
                rs_t[:, 2:3], rs_t[:, 0:1], 1.0 / D, ALU.mult), inc="cve")
            P.wait(VE, t3)
            t5 = P.run(VE, lambda E: E.tensor_single_scalar(
                rs_t[:, 3:4], rs_t[:, 1:2], 1.0 / D, ALU.mult), inc="cve")
            P.wait(VE, t4)
            t6 = P.run(VE, lambda E: E.tensor_mul(
                rs_t[:, 4:5], rs_t[:, 2:3], rs_t[:, 2:3]), inc="cve")
            P.wait(VE, t5)
            P.wait(VE, t6)
            t7 = P.run(VE, lambda E: E.tensor_sub(
                rs_t[:, 5:6], rs_t[:, 3:4], rs_t[:, 4:5]), inc="cve")
            P.wait(VE, t7)
            t7b = P.run(VE, lambda E: E.tensor_single_scalar(
                rs_t[:, 5:6], rs_t[:, 5:6], 1e-5, ALU.add), inc="cve")
            P.wait(SE, t7b)
            t8 = P.run(SE, lambda E: E.activation(
                rs_t[:, 6:7], rs_t[:, 5:6], AF.Sqrt), inc="cse")
            P.wait(VE, t8)
            t9 = P.run(VE, lambda E: E.reciprocal(rs_t[:, 7:8], rs_t[:, 6:7]),
                       inc="cve")
            P.wait(VE, t9)
            P.rd(VE, f"yacc{g}")
            P.wr(VE, OKEY[gs])
            ta = P.run(VE, lambda E, g=g, gs=gs: E.tensor_scalar(
                outst[gs][:, :], yacc[:, g * D:(g + 1) * D], rs_t[:, 2:3],
                rs_t[:, 7:8], ALU.subtract, ALU.mult), inc="cve")
            P.rdone(ta, f"yacc{g}")
            P.wdone(ta, OKEY[gs])
            P.rd(SC, OKEY[gs])
            tk = P.run(SC, lambda E, r=r, g=g, gs=gs: E.dma_start(
                out=out_d[r * TB + g * 128:r * TB + g * 128 + 128, :],
                in_=outst[gs][:, :]), inc=f"qout{gs}", n=16)
            P.rdone(tk, OKEY[gs])
            yield

    def stripe_for_round(r):
        need = (2 * r + 2) * TB
        s = (need + SST - 1) // SST - 1
        return min(s, ns - 1)

    scr_toks = []
    stage_state = [0]

    def flush_stages(n):
        while stage_state[0] < NE and n > 0:
            a1, a2, a3, a4 = stage_chunks[stage_state[0]]
            dma(PL, a1[:, :], a2[:, :], "qstgE")
            dma(PL, a3[:, :], a4[:, :], "qstgE")
            stage_state[0] += 1
            n -= 1

    s_i, r_i = 0, 0
    sgen, rgen = None, None
    while s_i < ns or r_i < nrounds or sgen is not None or rgen is not None:
        if sgen is None and s_i < ns:
            flush_stages(2)
            sgen = emit_stripe(s_i, stripes[s_i][0], stripes[s_i][1], scr_toks)
        if rgen is None and r_i < nrounds and                 len(scr_toks) > stripe_for_round(r_i):
            flush_stages(NE)
            rgen = emit_round(r_i, scr_toks)
        for _ in range(2):
            if rgen is not None:
                try:
                    next(rgen)
                except StopIteration:
                    rgen = None
                    r_i += 1
        if sgen is not None:
            try:
                next(sgen)
            except StopIteration:
                sgen = None
                s_i += 1
        if sgen is None and rgen is None and s_i >= ns and r_i < nrounds and                 len(scr_toks) <= stripe_for_round(r_i):
            break

    for sl in range(2):
        if f"qout{sl}" in P.cnt:
            P.wait(SY, (f"qout{sl}", P.cnt[f"qout{sl}"]))

    sems = {}
    for name in P.cnt:
        sems[name] = es.enter_context(nc.semaphore(name))
    blk = es.enter_context(nc.Block())
    with es:
        def runner(eng_name):
            def f(E):
                for op in P.ops[eng_name]:
                    if op[0] == "w":
                        E.wait_ge(sems[op[1]], op[2])
                    elif op[0] == "i":
                        E.sem_inc(sems[op[1]], op[2])
                    else:
                        _, fn, incn, n = op
                        ins = fn(E)
                        if incn is not None:
                            ins.then_inc(sems[incn], n)
            return f

        blk.sync(runner("sync"))
        blk.scalar(runner("scalar"))
        blk.vector(runner("vector"))
        blk.tensor(runner("tensor"))
        blk.gpsimd(runner("gpsimd"))
    return nc


def _inmaps(inputs, L):
    WIN = L // 2
    nblocks = L // TB
    x4 = np.ascontiguousarray(inputs["L0_attn_x"][:, :L])
    cw = lambda w: np.ascontiguousarray(
        np.transpose(w[:, 0, :].reshape(4, 8, 128), (2, 1, 0)).reshape(128, 32))
    vec8 = lambda v: np.ascontiguousarray(v.reshape(8, 128).T)
    be1c = np.ascontiguousarray(
        np.transpose(inputs["L0_ffn_be1"].reshape(NE, 32, 128), (2, 0, 1)
                     ).reshape(128, NE * 32))
    shared = dict(
        W_in=inputs["L0_attn_W_in"],
        cwx=cw(inputs["L0_attn_conv_x_w"]), cbx=vec8(inputs["L0_attn_conv_x_b"]),
        cwz=cw(inputs["L0_attn_conv_z_w"]), cbz=vec8(inputs["L0_attn_conv_z_b"]),
        W_xdbl=inputs["L0_attn_W_xdbl"], W_dt=inputs["L0_attn_W_dt"],
        invdt=vec8(inputs["L0_attn_inv_dt"]), Dv=vec8(inputs["L0_attn_D"]),
        W_out=inputs["L0_attn_W_out"], bout=vec8(inputs["L0_attn_b_out"]),
        rw1=inputs["L0_ffn_rw1"], rb1=vec8(inputs["L0_ffn_rb1"]),
        rw2=inputs["L0_ffn_rw2"],
        rb2=np.ascontiguousarray(inputs["L0_ffn_rb2"].reshape(NE, 1)),
        we1=inputs["L0_ffn_we1"], be1c=be1c,
        we2=inputs["L0_ffn_we2"], be2=inputs["L0_ffn_be2"],
        ident=np.eye(128, dtype=np.float32),
    )
    shared = {k: np.ascontiguousarray(v, dtype=np.float32)
              for k, v in shared.items()}
    maps = []
    for c in range(8):
        p, m = c // 2, c % 2
        blocks = np.arange(m, nblocks, 2)
        idx = np.ascontiguousarray(
            (blocks[:, None, None] * TB + np.arange(4)[None, :, None] * 128
             + np.arange(128)[None, None, :]).reshape(-1, 128).T.astype(np.int32))
        xres = np.ascontiguousarray(
            x4[p].reshape(nblocks, TB, D)[m::2].reshape(WIN, D))
        mp = dict(shared)
        mp["x"] = np.ascontiguousarray(x4[p])
        mp["xres"] = xres
        mp["idx"] = idx
        maps.append(mp)
    return maps


def kernel(**inputs):
    L = inputs["L0_attn_x"].shape[1]
    WIN = L // 2
    nblocks = L // TB
    nc = build(L)
    maps = _inmaps(inputs, L)
    res = run_bass_kernel_spmd(nc, maps, core_ids=list(range(8)))
    out = np.empty((4, L, D), dtype=np.float32)
    for c in range(8):
        p, m = c // 2, c % 2
        o = np.asarray(res.results[c]["out"], dtype=np.float32)
        out[p].reshape(nblocks, TB, D)[m::2] = o.reshape(nblocks // 2, TB, D)
    return out


# version 13
# speedup vs baseline: 1.4332x; 1.2010x over previous
import contextlib
import numpy as np

import concourse.bass as bass
import concourse.mybir as mybir
from concourse.bass_utils import run_bass_kernel_spmd
from concourse.bass_core_rust import IndirectOffsetOnAxis

F32 = mybir.dt.float32
BF16 = mybir.dt.bfloat16
I32 = mybir.dt.int32
AF = mybir.ActivationFunctionType
ALU = mybir.AluOpType
AX = mybir.AxisListType.X

D, DI, NST, DTR, NE, HE = 1024, 2048, 16, 64, 8, 4096
SW = 512
SST = SW - 3
TB = 512
ENGS = ("sync", "scalar", "vector", "tensor", "gpsimd")


class Pb:

    def __init__(self):
        self.ops = {e: [] for e in ENGS}
        self.cnt = {}
        self.hw = {}
        self.raw = {}
        self.war = {}

    def _sem(self, name):
        self.cnt.setdefault(name, 0)

    def wait(self, eng, tok):
        if tok is None:
            return
        s, v = tok
        if self.hw.get((eng, s), 0) >= v:
            return
        self.hw[(eng, s)] = v
        self.ops[eng].append(("w", s, v))

    def run(self, eng, fn, inc=None, n=1):
        tok = None
        if inc is not None:
            self._sem(inc)
            self.cnt[inc] += n
            tok = (inc, self.cnt[inc])
        self.ops[eng].append(("r", fn, inc, n))
        return tok

    def mark(self, eng, sem, n=1):
        self._sem(sem)
        self.cnt[sem] += n
        self.ops[eng].append(("i", sem, n))
        return (sem, self.cnt[sem])

    def rd(self, eng, *keys):
        for k in keys:
            self.wait(eng, self.raw.get(k))

    def rdone(self, tok, *keys):
        if tok is None:
            return
        for k in keys:
            self.war.setdefault(k, []).append(tok)

    def wr(self, eng, *keys):
        for k in keys:
            for t in self.war.get(k, []):
                self.wait(eng, t)
            self.wait(eng, self.raw.get(k))

    def wdone(self, tok, *keys):
        for k in keys:
            self.raw[k] = tok
            self.war[k] = []


def _stripes(L):
    out = []
    t0 = 0
    while t0 < L:
        out.append((t0, min(SST, L - t0)))
        t0 += SST
    return out


def build(L):
    WIN = L // 2
    nblocks = L // TB
    nrounds = nblocks // 2
    stripes = _stripes(L)
    ns = len(stripes)
    NWG = WIN // 128

    nc = bass.Bass("TRN2", target_bir_lowering=False)

    x_d = nc.dram_tensor("x", [L, D], F32, kind="ExternalInput").ap()
    xres_d = nc.dram_tensor("xres", [WIN, D], F32, kind="ExternalInput").ap()
    idx_d = nc.dram_tensor("idx", [128, NWG], I32, kind="ExternalInput").ap()
    Win_d = nc.dram_tensor("W_in", [D, DI], F32, kind="ExternalInput").ap()
    cwx_d = nc.dram_tensor("cwx", [128, 32], F32, kind="ExternalInput").ap()
    cbx_d = nc.dram_tensor("cbx", [128, 8], F32, kind="ExternalInput").ap()
    cwz_d = nc.dram_tensor("cwz", [128, 32], F32, kind="ExternalInput").ap()
    cbz_d = nc.dram_tensor("cbz", [128, 8], F32, kind="ExternalInput").ap()
    Wxd_d = nc.dram_tensor("W_xdbl", [D, 96], F32, kind="ExternalInput").ap()
    Wdt_d = nc.dram_tensor("W_dt", [DTR, D], F32, kind="ExternalInput").ap()
    inv_d = nc.dram_tensor("invdt", [128, 8], F32, kind="ExternalInput").ap()
    Dv_d = nc.dram_tensor("Dv", [128, 8], F32, kind="ExternalInput").ap()
    Wout_d = nc.dram_tensor("W_out", [DI, D], F32, kind="ExternalInput").ap()
    bout_d = nc.dram_tensor("bout", [128, 8], F32, kind="ExternalInput").ap()
    rw1_d = nc.dram_tensor("rw1", [D, D], F32, kind="ExternalInput").ap()
    rb1_d = nc.dram_tensor("rb1", [128, 8], F32, kind="ExternalInput").ap()
    rw2_d = nc.dram_tensor("rw2", [D, NE], F32, kind="ExternalInput").ap()
    rb2_d = nc.dram_tensor("rb2", [NE, 1], F32, kind="ExternalInput").ap()
    we1_d = nc.dram_tensor("we1", [NE, D, HE], F32, kind="ExternalInput").ap()
    be1_d = nc.dram_tensor("be1c", [128, NE * 32], F32, kind="ExternalInput").ap()
    we2_d = nc.dram_tensor("we2", [NE, HE, D], F32, kind="ExternalInput").ap()
    be2_d = nc.dram_tensor("be2", [NE, D], F32, kind="ExternalInput").ap()
    id_d = nc.dram_tensor("ident", [128, 128], F32, kind="ExternalInput").ap()
    out_d = nc.dram_tensor("out", [WIN, D], F32, kind="ExternalOutput").ap()

    att_s = nc.dram_tensor("att_s", [L, D], BF16, kind="Internal").ap()
    xdb_s = nc.dram_tensor("xdb_s", [32, L], F32, kind="Internal").ap()
    win_s = nc.dram_tensor("win_s", [D, DI], BF16, kind="Internal").ap()
    wout_s = nc.dram_tensor("wout_s", [DI, D], BF16, kind="Internal").ap()
    rw1_s = nc.dram_tensor("rw1_s", [D, D], BF16, kind="Internal").ap()
    we1_s = nc.dram_tensor("we1_s", [NE, D, HE], BF16, kind="Internal").ap()
    we2_s = nc.dram_tensor("we2_s", [NE, HE, D], BF16, kind="Internal").ap()

    es = contextlib.ExitStack()
    sb = lambda nm, sh, dt: es.enter_context(nc.sbuf_tensor(nm, sh, dt))

    c_idx = sb("c_idx", [128, NWG], I32)
    c_cwx = sb("c_cwx", [128, 32], F32)
    c_cbx = sb("c_cbx", [128, 8], F32)
    c_cwz = sb("c_cwz", [128, 32], F32)
    c_cbz = sb("c_cbz", [128, 8], F32)
    c_inv = sb("c_inv", [128, 8], F32)
    c_inv2 = sb("c_inv2", [128, 8], F32)
    c_D = sb("c_D", [128, 8], F32)
    c_bout = sb("c_bout", [128, 8], F32)
    c_rb1 = sb("c_rb1", [128, 8], F32)
    c_rb2 = sb("c_rb2", [NE, 1], F32)
    c_be1 = sb("c_be1", [128, NE * 32], F32)
    c_id32 = sb("c_id32", [128, 128], F32)
    c_idbf = sb("c_idbf", [128, 128], BF16)
    w_xdbl = sb("w_xdbl", [128, 8 * 96], BF16)
    w_dt = sb("w_dt", [DTR, D], BF16)
    w_rw2 = sb("w_rw2", [128, 8 * NE], BF16)
    w_be2 = sb("w_be2", [NE, D], BF16)

    xst = [sb(f"xst{i}", [128, 1024], BF16) for i in range(2)]
    attm = [sb("attm0", [128, 1024], BF16)]
    xfm = sb("xfm", [128, 8 * SW], BF16)
    xsb = sb("xsb", [128, 8 * SW], BF16)
    zb = sb("zb", [128, 8 * SW], BF16)
    ub = sb("ub", [128, 8 * SST], BF16)
    zcb = sb("zcb", [128, 8 * SST], BF16)
    cacc = [sb("cacc0", [128, SW], BF16), None]
    rvt = sb("rvt", [128, SST], F32)
    dtmp = [sb("dtmp0", [128, SST], F32), None]
    a_r = [sb(f"a_r{i}", [128, 2 * SST], F32) for i in range(2)]
    b_r = [sb(f"b_r{i}", [128, 2 * SST], BF16) for i in range(2)]
    h_r = [sb(f"h_r{i}", [128, 2 * SST], BF16) for i in range(3)]
    t_r = [sb(f"t_r{i}", [128, 2 * SST], BF16) for i in range(2)]
    bc_r = [sb(f"bc_r{i}", [128, 1024], BF16) for i in range(2)]
    yfm = sb("yfm", [128, 8 * SST], F32)
    carry = [sb(f"carry{i}", [128, 128], F32) for i in range(2)]
    xdbl = sb("xdbl", [128, SW], F32)
    dtlow = sb("dtlow", [DTR, SW], BF16)
    winb = [sb(f"winb{i}", [128, DI], BF16) for i in range(2)]
    woutb = [sb(f"woutb{i}", [128, D], BF16) for i in range(2)]

    attg = [sb("attg0", [128, D], BF16)]
    xresb = [sb("xresb0", [128, D], F32)]
    yacc = sb("yacc", [128, 4 * D], BF16)
    xtf = sb("xtf", [128, 8 * TB], BF16)
    hbuf = sb("hbuf", [128, 32 * TB], BF16)
    w1b = [sb(f"w1b{i}", [128, 8 * TB], BF16) for i in range(2)]
    w2b = [sb(f"w2b{i}", [128, D], BF16) for i in range(3)]
    lgf = sb("lgf", [NE, TB], F32)
    lgt = sb("lgt", [128, 32], F32)
    wtm = sb("wtm", [128, 32], F32)
    wfmb = sb("wfmb", [NE, TB], BF16)
    rs_t = sb("rs_t", [128, 16], F32)
    eqt = sb("eqt", [128, NE], F32)
    ett = sb("ett", [128, NE], F32)
    ewt = sb("ewt", [128, NE], F32)
    outst = [sb(f"outst{i}", [128, D], F32) for i in range(2)]

    pbk = [es.enter_context(nc.psum_tensor(f"pb{i}", [128, 512], F32))
           for i in range(8)]
    pbf = [p.bitcast(BF16) for p in pbk]

    OKEY = ["outst0", "outst1"]

    P = Pb()
    TE, VE, SE, PL, SY, SC = "tensor", "vector", "scalar", "gpsimd", "sync", "scalar"

    def dma(eng, dst, src, sem, key=None, waits=()):
        for t in waits:
            P.wait(eng, t)
        if key:
            P.wr(eng, key)
        tok = P.run(eng, lambda E, d=dst, s=src: E.dma_start(out=d, in_=s),
                    inc=sem, n=16)
        if key:
            P.wdone(tok, key)
        return tok

    consts = [
        (c_idx, idx_d), (c_cwx, cwx_d), (c_cbx, cbx_d), (c_cwz, cwz_d),
        (c_cbz, cbz_d), (c_inv, inv_d), (c_D, Dv_d), (c_bout, bout_d),
        (c_rb1, rb1_d), (c_rb2, rb2_d), (c_be1, be1_d), (c_id32, id_d),
    ]
    for t, s in consts:
        dma(SY, t[:, :], s[:, :], "qconst")
    tok_const = ("qconst", P.cnt["qconst"])

    for k in range(8):
        dma(PL, w_xdbl[:, k * 96:(k + 1) * 96], Wxd_d[k * 128:(k + 1) * 128, :],
            "qsmall")
    dma(PL, w_dt[:, :], Wdt_d[:, :], "qsmall")
    for k in range(8):
        dma(PL, w_rw2[:, k * NE:(k + 1) * NE], rw2_d[k * 128:(k + 1) * 128, :],
            "qsmall")
    dma(PL, w_be2[:, :], be2_d[:, :], "qsmall")
    tok_small = ("qsmall", P.cnt["qsmall"])

    dma(PL, win_s[:, :], Win_d[:, :], "qstgS")
    dma(PL, wout_s[:, :], Wout_d[:, :], "qstgS")
    dma(PL, rw1_s[:, :], rw1_d[:, :], "qstgS")
    tok_stgS = ("qstgS", P.cnt["qstgS"])
    stage_chunks = [(we1_s[e], we1_d[e], we2_s[e], we2_d[e]) for e in range(NE)]

    P.wait(VE, tok_const)
    t = P.run(VE, lambda E: E.tensor_add(c_inv2[:, :], c_inv[:, :], c_inv[:, :]),
              inc="cve")
    P.wdone(t, "c_inv2")
    P.wait(SE, tok_const)
    t = P.run(SE, lambda E: E.copy(c_idbf[:, :], c_id32[:, :]), inc="cse")
    P.wdone(t, "c_idbf")

    def emit_stripe(s, t0, slen, scr_out):
        W = slen + 3
        ngr = (W + 127) // 128

        for g in range(ngr):
            gs = g % 2
            tk0 = t0 - 1 + g * 128
            r0, r1 = max(tk0, 0), min(tk0 + 128, L)
            p0 = r0 - tk0
            part = (p0 != 0) or (r1 - r0 < 128)
            if part:
                P.wr(VE, f"xst{gs}")
                tm0 = P.run(VE, lambda E, gs=gs: E.memset(xst[gs][:, :], 0.0),
                            inc="cve")
                P.wdone(tm0, f"xst{gs}")
            P.wr(PL, f"xst{gs}")
            tok = P.run(PL, lambda E, r0=r0, r1=r1, p0=p0, gs=gs: E.dma_start(
                out=xst[gs][p0:p0 + (r1 - r0), :], in_=x_d[r0:r1, :]),
                inc=f"qx{gs}", n=16)
            P.wdone(tok, f"xst{gs}")
            P.wait(TE, tok)
            P.rd(TE, f"xst{gs}", "c_idbf")
            P.wr(TE, "pb7")
            for dt in range(8):
                tt = P.run(TE, lambda E, dt=dt, gs=gs: E.matmul(
                    pbf[7][:, dt * 128:(dt + 1) * 128],
                    xst[gs][:, dt * 128:(dt + 1) * 128],
                    c_idbf[:, :], start=(dt == 0), stop=(dt == 7),
                    is_transpose=True), inc="cte" if dt == 7 else None)
            P.wdone(tt, "pb7")
            P.rdone(tt, f"xst{gs}")
            gw = min(W - g * 128, 128)
            for dt in range(8):
                P.rd(SE, "pb7")
                P.wr(SE, f"xfm{dt}")
                te = P.run(SE, lambda E, dt=dt, g=g, gw=gw: E.copy(
                    xfm[:, dt * SW + g * 128:dt * SW + g * 128 + gw],
                    pbf[7][:, dt * 128:dt * 128 + gw]), inc="cse")
                P.rdone(te, "pb7")
                P.wdone(te, f"xfm{dt}")
            yield

        P.wait(SY, tok_stgS)
        for mh in range(2):
            for k in range(8):
                kb = k % 2
                dma(SY, winb[kb][:, :], win_s[k * 128:(k + 1) * 128, :],
                    f"qwin{kb}", key=f"winb{kb}")
                P.rd(TE, f"winb{kb}")
                if k == 0:
                    P.rd(TE, *[f"xfm{d}" for d in range(8)])
                for m8 in range(8):
                    m = mh * 8 + m8
                    if k == 0:
                        P.wr(TE, f"pb{m8}")
                    tt = P.run(TE, lambda E, k=k, m=m, kb=kb, m8=m8, W=W:
                               E.matmul(
                        pbk[m8][:, 0:W],
                        winb[kb][:, m * 128:(m + 1) * 128],
                        xfm[:, k * SW:k * SW + W],
                        start=(k == 0), stop=(k == 7)),
                        inc="cte" if (m8 == 7) else None)
                    if k == 7:
                        P.wdone((("cte", P.cnt["cte"] + 1)
                                 if m8 < 7 else tt), f"pb{m8}")
                P.rdone(tt, f"winb{kb}")
                for d in range(8):
                    P.rdone(tt, f"xfm{d}")
            for m8 in range(8):
                m = mh * 8 + m8
                dst = xsb if m < 8 else zb
                dkey = f"{'xsb' if m < 8 else 'zb'}{m % 8}"
                dcol = (m % 8) * SW
                P.rd(SE, f"pb{m8}")
                P.wr(SE, dkey)
                te = P.run(SE, lambda E, dst=dst, dcol=dcol, m8=m8, W=W: E.copy(
                    dst[:, dcol:dcol + W], pbk[m8][:, 0:W]), inc="cse")
                P.rdone(te, f"pb{m8}")
                P.wdone(te, dkey)
            yield

        for path, src, wsc, bsc, dst, dkey in (
                ("x", xsb, c_cwx, c_cbx, ub, "ub"),
                ("z", zb, c_cwz, c_cbz, zcb, "zcb")):
            skey = "xsb" if path == "x" else "zb"
            for dt in range(8):
                ca = cacc[0]
                ckey = "cacc0"
                P.rd(VE, f"{skey}{dt}")
                P.wait(VE, tok_const)
                P.wr(VE, ckey)
                tv = P.run(VE, lambda E, src=src, dt=dt, wsc=wsc, slen=slen,
                           ca=ca: E.tensor_scalar_mul(
                    ca[:, 0:slen], src[:, dt * SW:dt * SW + slen],
                    wsc[:, dt * 4:dt * 4 + 1]), inc="cve")
                P.wdone(tv, ckey)
                for j in range(1, 4):
                    P.wr(VE, ckey)
                    tv = P.run(VE, lambda E, src=src, dt=dt, j=j, wsc=wsc,
                               slen=slen, ca=ca: E.scalar_tensor_tensor(
                        ca[:, 0:slen],
                        src[:, dt * SW + j:dt * SW + j + slen],
                        wsc[:, dt * 4 + j:dt * 4 + j + 1],
                        ca[:, 0:slen], ALU.mult, ALU.add), inc="cve")
                    P.wdone(tv, ckey)
                P.rdone(tv, f"{skey}{dt}")
                P.rd(SE, ckey)
                P.wr(SE, f"{dkey}{dt}")
                te = P.run(SE, lambda E, dst=dst, dt=dt, bsc=bsc, slen=slen,
                           ca=ca: E.activation(
                    dst[:, dt * SST:dt * SST + slen], ca[:, 0:slen], AF.Silu,
                    bias=bsc[:, dt:dt + 1]), inc="cse")
                P.rdone(te, ckey)
                P.wdone(te, f"{dkey}{dt}")
                if dt % 4 == 3:
                    yield

        P.wait(TE, tok_small)
        for k in range(8):
            P.rd(TE, f"ub{k}")
            if k == 0:
                P.wr(TE, "pb0")
            tt = P.run(TE, lambda E, k=k, slen=slen: E.matmul(
                pbk[0][0:96, 0:slen], w_xdbl[:, k * 96:(k + 1) * 96],
                ub[:, k * SST:k * SST + slen],
                start=(k == 0), stop=(k == 7)), inc="cte")
            P.rdone(tt, f"ub{k}")
        P.wdone(tt, "pb0")
        P.rd(VE, "pb0")
        P.wr(VE, "xdbl")
        tv = P.run(VE, lambda E, slen=slen: E.tensor_copy(
            xdbl[0:96, 0:slen], pbk[0][0:96, 0:slen]), inc="cve")
        P.rdone(tv, "pb0")
        P.wdone(tv, "xdbl")
        P.rd(SE, "xdbl")
        P.wr(SE, "dtlow")
        te = P.run(SE, lambda E, slen=slen: E.copy(
            dtlow[:, 0:slen], xdbl[0:DTR, 0:slen]), inc="cse")
        P.wdone(te, "dtlow")
        P.rd(SC, "xdbl")
        tk = P.run(SC, lambda E, t0=t0, slen=slen: E.dma_start(
            out=xdb_s[:, t0:t0 + slen], in_=xdbl[64:96, 0:slen]),
            inc="qxdbl", n=16)
        P.rdone(tk, "xdbl")
        P.wait(SC, tk)
        tok_xd = P.mark(SC, "cxd")
        yield

        for m in range(8):
            bank = 1 + (m % 2)
            P.rd(TE, "dtlow")
            P.wr(TE, f"pb{bank}")
            tt = P.run(TE, lambda E, m=m, bank=bank, slen=slen: E.matmul(
                pbk[bank][:, 0:slen], w_dt[:, m * 128:(m + 1) * 128],
                dtlow[:, 0:slen], start=True, stop=True), inc="cte")
            P.rdone(tt, "dtlow")
            P.wdone(tt, f"pb{bank}")
            P.rd(SE, f"pb{bank}", "c_inv2")
            P.wr(SE, "dtmp0")
            ta = P.run(SE, lambda E, m=m, bank=bank, slen=slen: E.activation(
                dtmp[0][:, 0:slen], pbk[bank][:, 0:slen], AF.Abs,
                bias=c_inv2[:, m:m + 1]), inc="cse")
            P.wdone(ta, "dtmp0")
            P.wr(SE, "rvt")
            tr = P.run(SE, lambda E, m=m, bank=bank, slen=slen: E.activation(
                rvt[:, 0:slen], pbk[bank][:, 0:slen], AF.Relu,
                bias=c_inv2[:, m:m + 1]), inc="cse")
            P.rdone(tr, f"pb{bank}")
            P.wdone(tr, "rvt")
            P.rd(SE, "dtmp0")
            tx = P.run(SE, lambda E, slen=slen: E.activation(
                dtmp[0][:, 0:slen], dtmp[0][:, 0:slen], AF.Exp, scale=-1.0),
                inc="cse")
            P.wdone(tx, "dtmp0")
            P.rd(SE, "dtmp0")
            tl = P.run(SE, lambda E, slen=slen: E.activation(
                dtmp[0][:, 0:slen], dtmp[0][:, 0:slen], AF.Ln, bias=1.0),
                inc="cse")
            P.wdone(tl, "dtmp0")
            P.rd(VE, "dtmp0", "rvt")
            P.wr(VE, f"xfm{m}")
            tv = P.run(VE, lambda E, m=m, slen=slen: E.tensor_add(
                xfm[:, m * SW:m * SW + slen], rvt[:, 0:slen],
                dtmp[0][:, 0:slen]), inc="cve")
            P.rdone(tv, "dtmp0")
            P.rdone(tv, "rvt")
            P.wdone(tv, f"xfm{m}")
            if m % 4 == 3:
                yield

        for dt in range(8):
            P.rd(VE, f"ub{dt}")
            P.wr(VE, f"yfm{dt}")
            tv = P.run(VE, lambda E, dt=dt, slen=slen: E.tensor_scalar_mul(
                yfm[:, dt * SST:dt * SST + slen],
                ub[:, dt * SST:dt * SST + slen], c_D[:, dt:dt + 1]),
                inc="cve")
            P.rdone(tv, f"ub{dt}")
            P.wdone(tv, f"yfm{dt}")
            P.rd(VE, f"xfm{dt}", f"ub{dt}")
            P.wr(VE, f"ub{dt}")
            tv = P.run(VE, lambda E, dt=dt, slen=slen: E.tensor_mul(
                ub[:, dt * SST:dt * SST + slen],
                xfm[:, dt * SW:dt * SW + slen],
                ub[:, dt * SST:dt * SST + slen]), inc="cve")
            P.rdone(tv, f"xfm{dt}")
            P.wdone(tv, f"ub{dt}")

        for n in range(NST):
            bslot = n % 2
            bc = bc_r[bslot]
            P.wait(PL, tok_xd)
            P.wr(PL, f"bc{bslot}")
            tb1 = P.run(PL, lambda E, n=n, t0=t0, slen=slen, bc=bc: E.dma_start(
                out=bc[:, 0:slen],
                in_=xdb_s[n:n + 1, t0:t0 + slen].broadcast_to([128, slen])),
                inc=f"qbcB{bslot}", n=16)
            tb2 = P.run(PL, lambda E, n=n, t0=t0, slen=slen, bc=bc: E.dma_start(
                out=bc[:, 512:512 + slen],
                in_=xdb_s[16 + n:17 + n, t0:t0 + slen].broadcast_to([128, slen])),
                inc=f"qbcC{bslot}", n=16)
            P.wdone(tb2, f"bc{bslot}")
            P.raw[f"bcB{bslot}"] = tb1
            for j in range(4):
                i2 = n * 4 + j
                asl, hsl, tsl = i2 % 2, i2 % 3, i2 % 2
                d0, d1 = 2 * j, 2 * j + 1
                dpk = (f"ub{d0}", f"ub{d1}")
                dv = xfm[:, d0 * SW:(d1 + 1) * SW].rearrange(
                    "p (a m) -> p a m", a=2)[:, :, 0:slen]
                P.rd(SE, f"xfm{d0}", f"xfm{d1}")
                P.wr(SE, f"a{asl}")
                av = a_r[asl][:, 0:2 * SST].rearrange(
                    "p (a m) -> p a m", a=2)[:, :, 0:slen]
                ta = P.run(SE, lambda E, n=n, dv=dv, av=av: E.activation(
                    av, dv, AF.Exp, scale=-(n + 1.0)), inc="cse")
                P.wdone(ta, f"a{asl}")
                duv = ub[:, d0 * SST:d0 * SST + 2 * SST].rearrange(
                    "p (a m) -> p a m", a=2)[:, :, 0:slen]
                bbv = bc[:, 0:slen].rearrange("p (a m) -> p a m", a=1
                                              ).broadcast_to([128, 2, slen])
                bv = b_r[asl][:, 0:2 * SST].rearrange(
                    "p (a m) -> p a m", a=2)[:, :, 0:slen]
                P.rd(VE, *dpk, f"bc{bslot}", f"bcB{bslot}")
                P.wr(VE, f"b{asl}")
                tbv = P.run(VE, lambda E, duv=duv, bbv=bbv, bv=bv:
                            E.tensor_tensor(bv, duv, bbv, ALU.mult), inc="cve")
                P.rdone(tbv, *dpk)
                P.wdone(tbv, f"b{asl}")
                for q in range(2):
                    i = n * 8 + d0 + q
                    P.rd(VE, f"a{asl}", f"b{asl}")
                    if s > 0:
                        P.rd(VE, f"car{s % 2}_{i}")
                    P.wr(VE, f"h{hsl}")
                    init = 0.0 if s == 0 else carry[s % 2][:, i:i + 1]
                    th = P.run(VE, lambda E, asl=asl, hsl=hsl, q=q, init=init,
                               slen=slen: E.tensor_tensor_scan(
                        h_r[hsl][:, q * SST:q * SST + slen],
                        a_r[asl][:, q * SST:q * SST + slen],
                        b_r[asl][:, q * SST:q * SST + slen],
                        init, ALU.mult, ALU.add), inc="cve")
                    P.rdone(th, f"a{asl}")
                    P.rdone(th, f"b{asl}")
                    if s > 0:
                        P.rdone(th, f"car{s % 2}_{i}")
                    P.wdone(th, f"h{hsl}")
                if s + 1 < ns:
                    i = n * 8 + d0
                    hv = h_r[hsl][:, 0:2 * SST].rearrange(
                        "p (a m) -> p a m", a=2)[:, :, slen - 1:slen]
                    P.rd(VE, f"h{hsl}")
                    P.wr(VE, f"car{(s + 1) % 2}_{i}", f"car{(s + 1) % 2}_{i + 1}")
                    tc = P.run(VE, lambda E, hv=hv, i=i, s=s: E.tensor_copy(
                        carry[(s + 1) % 2][:, i:i + 2], hv), inc="cve")
                    P.wdone(tc, f"car{(s + 1) % 2}_{i}",
                            f"car{(s + 1) % 2}_{i + 1}")
                cbv = bc[:, 512:512 + slen].rearrange(
                    "p (a m) -> p a m", a=1).broadcast_to([128, 2, slen])
                hvv = h_r[hsl][:, 0:2 * SST].rearrange(
                    "p (a m) -> p a m", a=2)[:, :, 0:slen]
                tvv = t_r[tsl][:, 0:2 * SST].rearrange(
                    "p (a m) -> p a m", a=2)[:, :, 0:slen]
                P.rd(VE, f"h{hsl}", f"bc{bslot}")
                P.wr(VE, f"t{tsl}")
                tp = P.run(VE, lambda E, hvv=hvv, cbv=cbv, tvv=tvv:
                           E.tensor_tensor(tvv, hvv, cbv, ALU.mult), inc="cve")
                P.rdone(tp, f"h{hsl}")
                P.rdone(tp, f"bc{bslot}")
                P.wdone(tp, f"t{tsl}")
                yk = (f"yfm{d0}", f"yfm{d1}")
                P.rd(PL, f"t{tsl}", *yk)
                P.wr(PL, *yk)
                ty = P.run(PL, lambda E, d0=d0, tsl=tsl: E.tensor_tensor(
                    yfm[:, d0 * SST:(d0 + 2) * SST],
                    yfm[:, d0 * SST:(d0 + 2) * SST],
                    t_r[tsl][:, 0:2 * SST], ALU.add), inc="cpl")
                P.rdone(ty, f"t{tsl}")
                P.wdone(ty, *yk)
                yield

        for dt in range(8):
            P.rd(SE, f"yfm{dt}")
            P.wr(SE, f"xsb{dt}")
            te = P.run(SE, lambda E, dt=dt, slen=slen: E.copy(
                xsb[:, dt * SW:dt * SW + slen],
                yfm[:, dt * SST:dt * SST + slen]), inc="cse")
            P.rdone(te, f"yfm{dt}")
            P.wdone(te, f"xsb{dt}")
        yield

        for ki, k in enumerate(list(range(8, 16)) + list(range(8))):
            kb = ki % 2
            dma(SY, woutb[kb][:, :], wout_s[k * 128:(k + 1) * 128, :],
                f"qwout{kb}", key=f"woutb{kb}")
            P.rd(TE, f"woutb{kb}")
            if k < 8:
                P.rd(TE, f"xsb{k}")
            else:
                P.rd(TE, f"zcb{k - 8}")
            for m in range(8):
                if ki == 0:
                    P.wr(TE, f"pb{m}")
                rhs = (xsb[:, k * SW:k * SW + slen] if k < 8
                       else zcb[:, (k - 8) * SST:(k - 8) * SST + slen])
                tt = P.run(TE, lambda E, k=k, m=m, kb=kb, rhs=rhs, ki=ki:
                           E.matmul(
                    pbk[m][:, 0:slen], woutb[kb][:, m * 128:(m + 1) * 128],
                    rhs, start=(ki == 0), stop=(ki == 15)),
                    inc="cte" if m == 7 else None)
                if ki == 15:
                    P.wdone((("cte", P.cnt["cte"] + 1)
                             if m < 7 else tt), f"pb{m}")
            P.rdone(tt, f"woutb{kb}")
            if k < 8:
                P.rdone(tt, f"xsb{k}")
            else:
                P.rdone(tt, f"zcb{k - 8}")
        for m in range(8):
            P.rd(SE, f"pb{m}")
            P.wr(SE, f"zb{m}")
            te = P.run(SE, lambda E, m=m, slen=slen: E.activation(
                zb[:, m * SW:m * SW + slen], pbk[m][:, 0:slen], AF.Identity,
                bias=c_bout[:, m:m + 1]), inc="cse")
            P.rdone(te, f"pb{m}")
            P.wdone(te, f"zb{m}")

        ngo = (slen + 127) // 128
        for g in range(ngo):
            gw = min(128, slen - g * 128)
            aslot = 0
            akey = f"attm{aslot}"
            P.wr(TE, "pb7")
            for dt in range(8):
                P.rd(TE, f"zb{dt}")
                tt = P.run(TE, lambda E, dt=dt, g=g, gw=gw: E.matmul(
                    pbf[7][0:gw, dt * 128:dt * 128 + 128],
                    zb[:, dt * SW + g * 128:dt * SW + g * 128 + gw],
                    c_idbf[:, :], start=(dt == 0), stop=(dt == 7),
                    is_transpose=True), inc="cte")
                P.rdone(tt, f"zb{dt}")
            P.wdone(tt, "pb7")
            P.rd(VE, "pb7")
            P.wr(VE, akey)
            tv = P.run(VE, lambda E, aslot=aslot, gw=gw: E.tensor_copy(
                attm[aslot][0:gw, :], pbf[7][0:gw, :]), inc="cve")
            P.rdone(tv, "pb7")
            P.wdone(tv, akey)
            P.rd(SC, akey)
            tk = P.run(SC, lambda E, t0=t0, g=g, gw=gw, aslot=aslot:
                       E.dma_start(
                out=att_s[t0 + g * 128:t0 + g * 128 + gw, :],
                in_=attm[aslot][0:gw, :]), inc=f"qscr{aslot}", n=16)
            P.rdone(tk, akey)
            yield
        for sl in range(2):
            if f"qscr{sl}" in P.cnt:
                P.wait(SC, (f"qscr{sl}", P.cnt[f"qscr{sl}"]))
        scr_out.append(P.mark(SC, "cscr"))

    def emit_round(r, scr_list):
        for g in range(4):
            gs = g % 2
            need = (2 * r + 1) * TB + (g + 1) * 128
            sidx = min((need + SST - 1) // SST - 1, ns - 1)
            P.wait(PL, scr_list[sidx])
            P.wait(PL, tok_const)
            P.wr(PL, "attg0")
            tg = P.run(PL, lambda E, r=r, g=g: E.indirect_dma_start(
                attg[0][:, :], None, att_s[:, :],
                IndirectOffsetOnAxis(c_idx[:, r * 4 + g:r * 4 + g + 1], 0)),
                inc="qgat0", n=16)
            P.wdone(tg, "attg0")
            P.wr(SC, "xres0")
            tr = P.run(SC, lambda E, r=r, g=g: E.dma_start(
                out=xresb[0][:, :],
                in_=xres_d[r * TB + g * 128:r * TB + g * 128 + 128, :]),
                inc=f"qres{gs}", n=16)
            P.wdone(tr, "xres0")
            P.rd(VE, "attg0", "xres0")
            P.wr(VE, "xres0")
            tv = P.run(VE, lambda E: E.tensor_add(
                xresb[0][:, :], xresb[0][:, :], attg[0][:, :]), inc="cve")
            P.rdone(tv, "attg0")
            P.wdone(tv, "xres0")
            P.rd(VE, "xres0")
            t1 = P.run(VE, lambda E: E.reduce_sum(
                rs_t[:, 0:1], xresb[0][:, :], axis=AX), inc="cve")
            P.rd(SE, "xres0")
            P.wr(SE, OKEY[gs])
            t2 = P.run(SE, lambda E, gs=gs: E.activation(
                outst[gs][:, :], xresb[0][:, :], AF.Square), inc="cse")
            P.wdone(t2, OKEY[gs])
            P.rd(VE, OKEY[gs])
            t3 = P.run(VE, lambda E, gs=gs: E.reduce_sum(
                rs_t[:, 1:2], outst[gs][:, :], axis=AX), inc="cve")
            P.rdone(t3, OKEY[gs])
            P.wait(VE, t1)
            t4 = P.run(VE, lambda E: E.tensor_single_scalar(
                rs_t[:, 2:3], rs_t[:, 0:1], 1.0 / D, ALU.mult), inc="cve")
            P.wait(VE, t3)
            t5 = P.run(VE, lambda E: E.tensor_single_scalar(
                rs_t[:, 3:4], rs_t[:, 1:2], 1.0 / D, ALU.mult), inc="cve")
            P.wait(VE, t4)
            t6 = P.run(VE, lambda E: E.tensor_mul(
                rs_t[:, 4:5], rs_t[:, 2:3], rs_t[:, 2:3]), inc="cve")
            P.wait(VE, t5)
            P.wait(VE, t6)
            t7 = P.run(VE, lambda E: E.tensor_sub(
                rs_t[:, 5:6], rs_t[:, 3:4], rs_t[:, 4:5]), inc="cve")
            P.wait(VE, t7)
            t7b = P.run(VE, lambda E: E.tensor_single_scalar(
                rs_t[:, 5:6], rs_t[:, 5:6], 1e-5, ALU.add), inc="cve")
            P.wait(SE, t7b)
            t8 = P.run(SE, lambda E: E.activation(
                rs_t[:, 6:7], rs_t[:, 5:6], AF.Sqrt), inc="cse")
            P.wait(VE, t8)
            t9 = P.run(VE, lambda E: E.reciprocal(rs_t[:, 7:8], rs_t[:, 6:7]),
                       inc="cve")
            P.wait(VE, t9)
            ta = P.run(VE, lambda E: E.tensor_single_scalar(
                rs_t[:, 8:9], rs_t[:, 7:8], 2.0, ALU.mult), inc="cve")
            P.wait(VE, ta)
            P.rd(VE, "xres0")
            P.wr(VE, f"yacc{g}")
            tb3 = P.run(VE, lambda E, g=g: E.tensor_scalar(
                yacc[:, g * D:(g + 1) * D], xresb[0][:, :], rs_t[:, 2:3],
                rs_t[:, 8:9], ALU.subtract, ALU.mult), inc="cve")
            P.rdone(tb3, "xres0")
            P.wdone(tb3, f"yacc{g}")
            yield

        for dt in range(8):
            P.wr(TE, "pb7")
            for g in range(4):
                P.rd(TE, f"yacc{g}")
                tt = P.run(TE, lambda E, dt=dt, g=g: E.matmul(
                    pbf[7][:, g * 128:(g + 1) * 128],
                    yacc[:, g * D + dt * 128:g * D + dt * 128 + 128],
                    c_idbf[:, :], start=(g == 0), stop=(g == 3),
                    is_transpose=True), inc="cte")
                P.rdone(tt, f"yacc{g}")
            P.wdone(tt, "pb7")
            P.rd(SE, "pb7")
            P.wr(SE, f"xtf{dt}")
            te = P.run(SE, lambda E, dt=dt: E.copy(
                xtf[:, dt * TB:(dt + 1) * TB], pbf[7][:, 0:TB]), inc="cse")
            P.rdone(te, "pb7")
            P.wdone(te, f"xtf{dt}")
            if dt % 4 == 3:
                yield

        for k in range(8):
            kb = k % 3
            dma(SY, w2b[kb][:, :], rw1_s[k * 128:(k + 1) * 128, :],
                f"qw2{kb}", key=f"w2b{kb}", waits=(tok_stgS,))
            P.rd(TE, f"w2b{kb}", f"xtf{k}")
            for m in range(8):
                if k == 0:
                    P.wr(TE, f"pb{m}")
                tt = P.run(TE, lambda E, k=k, m=m, kb=kb: E.matmul(
                    pbk[m][:, 0:TB], w2b[kb][:, m * 128:(m + 1) * 128],
                    xtf[:, k * TB:(k + 1) * TB],
                    start=(k == 0), stop=(k == 7)),
                    inc="cte" if m == 7 else None)
                if k == 7:
                    P.wdone((("cte", P.cnt["cte"] + 1)
                             if m < 7 else tt), f"pb{m}")
            P.rdone(tt, f"w2b{kb}")
            P.rdone(tt, f"xtf{k}")
        for m in range(8):
            P.rd(SE, f"pb{m}")
            P.wr(SE, f"h{m}")
            te = P.run(SE, lambda E, m=m: E.activation(
                hbuf[:, m * TB:(m + 1) * TB], pbk[m][:, 0:TB], AF.Silu,
                bias=c_rb1[:, m:m + 1], scale=0.5), inc="cse")
            P.rdone(te, f"pb{m}")
            P.wdone(te, f"h{m}")
        P.wait(TE, tok_small)
        for k in range(8):
            P.rd(TE, f"h{k}")
            if k == 0:
                P.wr(TE, "pb0")
            tt = P.run(TE, lambda E, k=k: E.matmul(
                pbk[0][0:NE, 0:TB], w_rw2[:, k * NE:(k + 1) * NE],
                hbuf[:, k * TB:(k + 1) * TB], start=(k == 0), stop=(k == 7)),
                inc="cte")
            P.rdone(tt, f"h{k}")
        P.wdone(tt, "pb0")
        P.rd(VE, "pb0")
        P.wr(VE, "lgf")
        tv = P.run(VE, lambda E: E.tensor_scalar_add(
            lgf[:, :], pbk[0][0:NE, 0:TB], c_rb2[:, 0:1]), inc="cve")
        P.rdone(tv, "pb0")
        P.wdone(tv, "lgf")
        P.wr(TE, "pb1")
        P.rd(TE, "lgf")
        for g in range(4):
            tt = P.run(TE, lambda E, g=g: E.matmul(
                pbk[1][:, g * 8:g * 8 + 8], lgf[:, g * 128:(g + 1) * 128],
                c_id32[0:8, 0:8], start=(g == 0), stop=(g == 3),
                is_transpose=True), inc="cte" if g == 3 else None)
        P.rdone(tt, "lgf")
        P.wdone(tt, "pb1")
        P.rd(VE, "pb1")
        P.wr(VE, "lgt")
        tv = P.run(VE, lambda E: E.tensor_copy(lgt[:, :], pbk[1][:, 0:32]),
                   inc="cve")
        P.rdone(tv, "pb1")
        P.wdone(tv, "lgt")
        for g in range(4):
            lg = lgt[:, g * 8:(g + 1) * 8]
            P.rd(VE, "lgt")
            t1 = P.run(VE, lambda E, lg=lg: E.reduce_max(
                rs_t[:, 9:10], lg, axis=AX), inc="cve")
            P.wait(VE, t1)
            t2 = P.run(VE, lambda E, lg=lg: E.tensor_single_scalar(
                eqt[:, :], lg, rs_t[:, 9:10], ALU.is_equal), inc="cve")
            P.wait(VE, t2)
            t3 = P.run(VE, lambda E, lg=lg: E.scalar_tensor_tensor(
                ewt[:, :], eqt[:, :], -1e30, lg, ALU.mult, ALU.add),
                inc="cve")
            P.wait(VE, t3)
            t4 = P.run(VE, lambda E: E.reduce_max(
                rs_t[:, 10:11], ewt[:, :], axis=AX), inc="cve")
            t5 = P.run(VE, lambda E: E.tensor_single_scalar(
                rs_t[:, 11:12], rs_t[:, 9:10], -1.0, ALU.mult), inc="cve")
            P.wait(SE, t5)
            t6 = P.run(SE, lambda E, lg=lg: E.activation(
                ett[:, :], lg, AF.Exp, bias=rs_t[:, 11:12]), inc="cse")
            P.wait(VE, t4)
            t7 = P.run(VE, lambda E, lg=lg: E.tensor_single_scalar(
                eqt[:, :], lg, rs_t[:, 10:11], ALU.is_ge), inc="cve")
            P.wait(VE, t6)
            P.wait(VE, t7)
            t8 = P.run(VE, lambda E: E.tensor_mul(
                ewt[:, :], ett[:, :], eqt[:, :]), inc="cve")
            P.wait(VE, t8)
            t9 = P.run(VE, lambda E: E.reduce_sum(
                rs_t[:, 12:13], ewt[:, :], axis=AX), inc="cve")
            P.wait(VE, t9)
            ta = P.run(VE, lambda E: E.reciprocal(
                rs_t[:, 13:14], rs_t[:, 12:13]), inc="cve")
            P.wait(VE, ta)
            P.wr(VE, f"wtm{g}")
            tb3 = P.run(VE, lambda E, g=g: E.tensor_scalar_mul(
                wtm[:, g * 8:(g + 1) * 8], ewt[:, :], rs_t[:, 13:14]),
                inc="cve")
            P.wdone(tb3, f"wtm{g}")
        P.wr(TE, "pb2")
        for g in range(4):
            P.rd(TE, f"wtm{g}")
            tt = P.run(TE, lambda E, g=g: E.matmul(
                pbk[2][0:8, g * 128:(g + 1) * 128], wtm[:, g * 8:(g + 1) * 8],
                c_id32[:, :], start=(g == 0), stop=(g == 3),
                is_transpose=True), inc="cte" if g == 3 else None)
            P.rdone(tt, f"wtm{g}")
        P.wdone(tt, "pb2")
        P.rd(SE, "pb2")
        P.wr(SE, "wfmb")
        te = P.run(SE, lambda E: E.copy(wfmb[:, :], pbk[2][0:8, 0:TB]),
                   inc="cse")
        P.rdone(te, "pb2")
        P.wdone(te, "wfmb")
        for g in range(4):
            for hh in range(2):
                bank = 3 + hh
                P.rd(TE, "wfmb")
                P.wr(TE, f"pb{bank}")
                tt = P.run(TE, lambda E, g=g, hh=hh, bank=bank: E.matmul(
                    pbk[bank][:, 0:TB], wfmb[:, g * 128:(g + 1) * 128],
                    w_be2[:, hh * TB:(hh + 1) * TB], start=True, stop=True),
                    inc="cte")
                P.rdone(tt, "wfmb")
                P.wdone(tt, f"pb{bank}")
                P.rd(VE, f"pb{bank}", f"yacc{g}")
                P.wr(VE, f"yacc{g}")
                tv = P.run(VE, lambda E, g=g, hh=hh, bank=bank: E.tensor_add(
                    yacc[:, g * D + hh * TB:g * D + (hh + 1) * TB],
                    yacc[:, g * D + hh * TB:g * D + (hh + 1) * TB],
                    pbk[bank][:, 0:TB]), inc="cve")
                P.rdone(tv, f"pb{bank}")
                P.wdone(tv, f"yacc{g}")
        yield

        for e in range(NE):
            for mg in range(8):
                mb = mg % 2
                src1 = we1_s[e][:, mg * TB:(mg + 1) * TB].rearrange(
                    "(k p) m -> p k m", p=128)
                for half in range(2):
                    dst = w1b[mb][:, half * 2048:(half + 1) * 2048].rearrange(
                        "p (k m) -> p k m", k=4)
                    dma(SY, dst, src1[:, half * 4:(half + 1) * 4, :],
                        f"qw1{mb}", key=f"w1b{mb}",
                        waits=(("qstgE", 16 * 2 * NE),))
                P.rd(TE, f"w1b{mb}")
                for mt in range(4):
                    bank = 5 + (mt % 2)
                    P.wr(TE, f"pb{bank}")
                    for k in range(8):
                        P.rd(TE, f"xtf{k}")
                        tt = P.run(TE, lambda E, mb=mb, mt=mt, k=k, bank=bank:
                                   E.matmul(
                            pbk[bank][:, 0:TB],
                            w1b[mb][:, k * TB + mt * 128:k * TB + mt * 128 + 128],
                            xtf[:, k * TB:(k + 1) * TB],
                            start=(k == 0), stop=(k == 7)),
                            inc="cte" if k == 7 else None)
                    P.wdone(tt, f"pb{bank}")
                    hm = mg * 4 + mt
                    P.rd(SE, f"pb{bank}")
                    P.wr(SE, f"h{hm}")
                    te = P.run(SE, lambda E, e=e, hm=hm, bank=bank:
                               E.activation(
                        hbuf[:, hm * TB:(hm + 1) * TB], pbk[bank][:, 0:TB],
                        AF.Silu, bias=c_be1[:, e * 32 + hm:e * 32 + hm + 1],
                        scale=0.5), inc="cse")
                    P.rdone(te, f"pb{bank}")
                    P.wdone(te, f"h{hm}")
                    yield
                P.rdone(tt, f"w1b{mb}")
            for tt4 in range(4):
                for hh in range(2):
                    P.wr(TE, f"pb{tt4 * 2 + hh}")
            for k in range(32):
                kb = k % 3
                dma(SY, w2b[kb][:, :], we2_s[e][k * 128:(k + 1) * 128, :],
                    f"qw2{kb}", key=f"w2b{kb}")
                P.rd(TE, f"w2b{kb}", f"h{k}")
                for tt4 in range(4):
                    for hh in range(2):
                        tt = P.run(TE, lambda E, k=k, kb=kb, tt4=tt4, hh=hh:
                                   E.matmul(
                            pbk[tt4 * 2 + hh][:, 0:TB],
                            hbuf[:, k * TB + tt4 * 128:k * TB + tt4 * 128 + 128],
                            w2b[kb][:, hh * TB:(hh + 1) * TB],
                            start=(k == 0), stop=(k == 31)),
                            inc="cte" if (tt4 == 3 and hh == 1) else None)
                P.rdone(tt, f"w2b{kb}")
                P.rdone(tt, f"h{k}")
            tlast = tt
            for tt4 in range(4):
                for hh in range(2):
                    bank = tt4 * 2 + hh
                    P.wait(VE, tlast)
                    P.rd(VE, f"pb{bank}", f"yacc{tt4}", f"wtm{tt4}")
                    P.wr(VE, f"yacc{tt4}")
                    tv = P.run(VE, lambda E, e=e, tt4=tt4, hh=hh, bank=bank:
                               E.scalar_tensor_tensor(
                        yacc[:, tt4 * D + hh * TB:tt4 * D + (hh + 1) * TB],
                        pbk[bank][:, 0:TB],
                        wtm[:, tt4 * 8 + e:tt4 * 8 + e + 1],
                        yacc[:, tt4 * D + hh * TB:tt4 * D + (hh + 1) * TB],
                        ALU.mult, ALU.add), inc="cve")
                    P.rdone(tv, f"pb{bank}")
                    P.wdone(tv, f"yacc{tt4}")
            yield

        for g in range(4):
            gs = g % 2
            P.rd(VE, f"yacc{g}")
            t1 = P.run(VE, lambda E, g=g: E.reduce_sum(
                rs_t[:, 0:1], yacc[:, g * D:(g + 1) * D], axis=AX), inc="cve")
            P.rd(SE, f"yacc{g}")
            P.wr(SE, OKEY[gs])
            t2 = P.run(SE, lambda E, g=g, gs=gs: E.activation(
                outst[gs][:, :], yacc[:, g * D:(g + 1) * D], AF.Square),
                inc="cse")
            P.wdone(t2, OKEY[gs])
            P.rd(VE, OKEY[gs])
            t3 = P.run(VE, lambda E, gs=gs: E.reduce_sum(
                rs_t[:, 1:2], outst[gs][:, :], axis=AX), inc="cve")
            P.rdone(t3, OKEY[gs])
            P.wait(VE, t1)
            t4 = P.run(VE, lambda E: E.tensor_single_scalar(
                rs_t[:, 2:3], rs_t[:, 0:1], 1.0 / D, ALU.mult), inc="cve")
            P.wait(VE, t3)
            t5 = P.run(VE, lambda E: E.tensor_single_scalar(
                rs_t[:, 3:4], rs_t[:, 1:2], 1.0 / D, ALU.mult), inc="cve")
            P.wait(VE, t4)
            t6 = P.run(VE, lambda E: E.tensor_mul(
                rs_t[:, 4:5], rs_t[:, 2:3], rs_t[:, 2:3]), inc="cve")
            P.wait(VE, t5)
            P.wait(VE, t6)
            t7 = P.run(VE, lambda E: E.tensor_sub(
                rs_t[:, 5:6], rs_t[:, 3:4], rs_t[:, 4:5]), inc="cve")
            P.wait(VE, t7)
            t7b = P.run(VE, lambda E: E.tensor_single_scalar(
                rs_t[:, 5:6], rs_t[:, 5:6], 1e-5, ALU.add), inc="cve")
            P.wait(SE, t7b)
            t8 = P.run(SE, lambda E: E.activation(
                rs_t[:, 6:7], rs_t[:, 5:6], AF.Sqrt), inc="cse")
            P.wait(VE, t8)
            t9 = P.run(VE, lambda E: E.reciprocal(rs_t[:, 7:8], rs_t[:, 6:7]),
                       inc="cve")
            P.wait(VE, t9)
            P.rd(VE, f"yacc{g}")
            P.wr(VE, OKEY[gs])
            ta = P.run(VE, lambda E, g=g, gs=gs: E.tensor_scalar(
                outst[gs][:, :], yacc[:, g * D:(g + 1) * D], rs_t[:, 2:3],
                rs_t[:, 7:8], ALU.subtract, ALU.mult), inc="cve")
            P.rdone(ta, f"yacc{g}")
            P.wdone(ta, OKEY[gs])
            P.rd(SC, OKEY[gs])
            tk = P.run(SC, lambda E, r=r, g=g, gs=gs: E.dma_start(
                out=out_d[r * TB + g * 128:r * TB + g * 128 + 128, :],
                in_=outst[gs][:, :]), inc=f"qout{gs}", n=16)
            P.rdone(tk, OKEY[gs])
            yield

    def stripe_for_round(r):
        need = (2 * r + 2) * TB
        s = (need + SST - 1) // SST - 1
        return min(s, ns - 1)

    scr_toks = []
    stage_state = [0]

    def flush_stages(n):
        while stage_state[0] < NE and n > 0:
            a1, a2, a3, a4 = stage_chunks[stage_state[0]]
            dma(PL, a1[:, :], a2[:, :], "qstgE")
            dma(PL, a3[:, :], a4[:, :], "qstgE")
            stage_state[0] += 1
            n -= 1

    s_i, r_i = 0, 0
    sgen, rgen = None, None
    while s_i < ns or r_i < nrounds or sgen is not None or rgen is not None:
        if sgen is None and s_i < ns:
            flush_stages(2)
            sgen = emit_stripe(s_i, stripes[s_i][0], stripes[s_i][1], scr_toks)
        if rgen is None and r_i < nrounds and                 len(scr_toks) > stripe_for_round(r_i):
            flush_stages(NE)
            rgen = emit_round(r_i, scr_toks)
        for _ in range(2):
            if rgen is not None:
                try:
                    next(rgen)
                except StopIteration:
                    rgen = None
                    r_i += 1
        if sgen is not None:
            try:
                next(sgen)
            except StopIteration:
                sgen = None
                s_i += 1
        if sgen is None and rgen is None and s_i >= ns and r_i < nrounds and                 len(scr_toks) <= stripe_for_round(r_i):
            break

    for sl in range(2):
        if f"qout{sl}" in P.cnt:
            P.wait(SY, (f"qout{sl}", P.cnt[f"qout{sl}"]))

    sems = {}
    for name in P.cnt:
        sems[name] = es.enter_context(nc.semaphore(name))
    blk = es.enter_context(nc.Block())
    with es:
        def runner(eng_name):
            def f(E):
                for op in P.ops[eng_name]:
                    if op[0] == "w":
                        E.wait_ge(sems[op[1]], op[2])
                    elif op[0] == "i":
                        E.sem_inc(sems[op[1]], op[2])
                    else:
                        _, fn, incn, n = op
                        ins = fn(E)
                        if incn is not None:
                            ins.then_inc(sems[incn], n)
            return f

        blk.sync(runner("sync"))
        blk.scalar(runner("scalar"))
        blk.vector(runner("vector"))
        blk.tensor(runner("tensor"))
        blk.gpsimd(runner("gpsimd"))
    return nc


def _inmaps(inputs, L):
    WIN = L // 2
    nblocks = L // TB
    x4 = np.ascontiguousarray(inputs["L0_attn_x"][:, :L])
    cw = lambda w: np.ascontiguousarray(
        np.transpose(w[:, 0, :].reshape(4, 8, 128), (2, 1, 0)).reshape(128, 32))
    vec8 = lambda v: np.ascontiguousarray(v.reshape(8, 128).T)
    be1c = np.ascontiguousarray(
        np.transpose(inputs["L0_ffn_be1"].reshape(NE, 32, 128), (2, 0, 1)
                     ).reshape(128, NE * 32))
    shared = dict(
        W_in=inputs["L0_attn_W_in"],
        cwx=cw(inputs["L0_attn_conv_x_w"]), cbx=vec8(inputs["L0_attn_conv_x_b"]),
        cwz=cw(inputs["L0_attn_conv_z_w"]), cbz=vec8(inputs["L0_attn_conv_z_b"]),
        W_xdbl=inputs["L0_attn_W_xdbl"], W_dt=inputs["L0_attn_W_dt"],
        invdt=vec8(inputs["L0_attn_inv_dt"]), Dv=vec8(inputs["L0_attn_D"]),
        W_out=inputs["L0_attn_W_out"], bout=vec8(inputs["L0_attn_b_out"]),
        rw1=inputs["L0_ffn_rw1"], rb1=vec8(inputs["L0_ffn_rb1"]),
        rw2=inputs["L0_ffn_rw2"],
        rb2=np.ascontiguousarray(inputs["L0_ffn_rb2"].reshape(NE, 1)),
        we1=inputs["L0_ffn_we1"], be1c=be1c,
        we2=inputs["L0_ffn_we2"], be2=inputs["L0_ffn_be2"],
        ident=np.eye(128, dtype=np.float32),
    )
    shared = {k: np.ascontiguousarray(v, dtype=np.float32)
              for k, v in shared.items()}
    maps = []
    for c in range(8):
        p, m = c // 2, c % 2
        blocks = np.arange(m, nblocks, 2)
        idx = np.ascontiguousarray(
            (blocks[:, None, None] * TB + np.arange(4)[None, :, None] * 128
             + np.arange(128)[None, None, :]).reshape(-1, 128).T.astype(np.int32))
        xres = np.ascontiguousarray(
            x4[p].reshape(nblocks, TB, D)[m::2].reshape(WIN, D))
        mp = dict(shared)
        mp["x"] = np.ascontiguousarray(x4[p])
        mp["xres"] = xres
        mp["idx"] = idx
        maps.append(mp)
    return maps


def kernel(**inputs):
    L = inputs["L0_attn_x"].shape[1]
    WIN = L // 2
    nblocks = L // TB
    nc = build(L)
    maps = _inmaps(inputs, L)
    res = run_bass_kernel_spmd(nc, maps, core_ids=list(range(8)))
    out = np.empty((4, L, D), dtype=np.float32)
    for c in range(8):
        p, m = c // 2, c % 2
        o = np.asarray(res.results[c]["out"], dtype=np.float32)
        out[p].reshape(nblocks, TB, D)[m::2] = o.reshape(nblocks // 2, TB, D)
    return out
